# version 24
# baseline (speedup 1.0000x reference)
import numpy as np
import ml_dtypes
import concourse.bass as bass
import concourse.mybir as mybir
from concourse.bass_utils import run_bass_kernel_spmd
from contextlib import ExitStack

BF = mybir.dt.bfloat16
F32 = mybir.dt.float32
AF = mybir.ActivationFunctionType
ALU = mybir.AluOpType
AXX = mybir.AxisListType.X

NCORE = 8
DM = 1024
H, HD = 16, 64
CHUNK = 128
NCHUNK = 32
NTOK = NCHUNK * CHUNK
SUP = 4
NSUP = NCHUNK // SUP
STOK = SUP * CHUNK
NSM = 80

bf16 = ml_dtypes.bfloat16
ENGS = ("sync", "scalar", "vector", "tensor", "gpsimd")


def set_dims(nchunk):
    global NCHUNK, NTOK, NSUP
    NCHUNK = nchunk
    NTOK = NCHUNK * CHUNK
    NSUP = NCHUNK // SUP
    assert NCHUNK % SUP == 0


class Prog:
    def __init__(self, nc, es):
        self.nc = nc
        self.sem = {e: es.enter_context(nc.semaphore(f"T{e}")) for e in ENGS}
        self.cnt = {e: 0 for e in ENGS}
        self.stream = {e: [] for e in ENGS}
        self.water = {e: {} for e in ENGS}
        self.dsem, self.dcnt = {}, {}
        self.es = es

    def add_dma_sem(self, name):
        for r in range(2):
            nm = f"{name}{r}"
            self.dsem[nm] = self.es.enter_context(self.nc.semaphore(f"D{nm}"))
            self.dcnt[nm] = 0

    def _wait(self, e, tok):
        key, sem, val = tok
        if self.water[e].get(key, 0) >= val:
            return
        self.water[e][key] = val
        self.stream[e].append(("w", sem, val))

    def _do_waits(self, e, waits):
        best = {}
        for t in waits:
            if t is None:
                continue
            k = t[0]
            if k not in best or t[2] > best[k][2]:
                best[k] = t
        for t in best.values():
            self._wait(e, t)

    def op(self, e, meth, *args, waits=(), inc=True, **kw):
        self._do_waits(e, waits)
        if inc:
            self.cnt[e] += 1
            self.stream[e].append(("i", meth, args, kw, self.sem[e], 1))
        else:
            self.stream[e].append(("o", meth, args, kw))
        return ("E" + e, self.sem[e], self.cnt[e])

    def dma(self, e, dname, out, in_, rot=0, fresh=True, waits=(), **kw):
        nm = f"{dname}{rot % 2}"
        s = self.dsem[nm]
        if fresh and self.dcnt[nm] > 0:
            self._wait(e, ("D" + nm, s, self.dcnt[nm]))
        self._do_waits(e, waits)
        self.dcnt[nm] += 16
        self.stream[e].append(("i", "dma_start", (), dict(out=out, in_=in_, **kw),
                               s, 16))
        return ("D" + nm, s, self.dcnt[nm])

    def tok(self, e):
        return ("E" + e, self.sem[e], self.cnt[e])

    def fence_all(self):
        toks = [self.tok(e) for e in ENGS] + \
            [("D" + d, self.dsem[d], self.dcnt[d]) for d in self.dsem]
        for e in ENGS:
            for t in toks:
                if t[2] > 0 and t[0] != "E" + e:
                    self._wait(e, t)

    def final_wait(self, e, dname):
        for r in range(2):
            nm = f"{dname}{r}"
            if self.dcnt[nm] > 0:
                self.stream[e].append(("w", self.dsem[nm], self.dcnt[nm]))

    def build(self, block):
        for e in ENGS:
            ops = self.stream[e]

            def runner(eng, ops=ops):
                for it in ops:
                    if it[0] == "w":
                        eng.wait_ge(it[1], it[2])
                    elif it[0] == "o":
                        getattr(eng, it[1])(*it[2], **it[3])
                    else:
                        getattr(eng, it[1])(*it[2], **it[3]).then_inc(it[4], it[5])
            getattr(block, e)(runner)


class Track:
    def __init__(self):
        self.w = {}
        self.r = {}

    def rw(self, name):
        return [self.w.get(name)]

    def ww(self, name):
        return self.r.get(name, []) + [self.w.get(name)]

    def did_read(self, name, tok):
        self.r.setdefault(name, []).append(tok)

    def did_write(self, name, tok):
        self.w[name] = tok
        self.r[name] = []


def build_nc(taps=(), use_cc=True, skip=()):
    skip = set(skip)
    taps = set(taps)
    nc = bass.Bass("TRN2", target_bir_lowering=False, debug=False)
    DT = nc.dram_tensor

    x_in = DT("x", [NTOK, DM], F32, kind="ExternalInput")
    w_in = {n: DT(n, [DM, DM], F32, kind="ExternalInput")
            for n in ("wq", "wk", "wv", "wo")}
    sm_in = {n: DT(n, [DM, H], F32, kind="ExternalInput")
             for n in ("wb", "wfd", "wsd", "wfg", "wsg")}
    fdb_in = DT("fdb", [1, H], F32, kind="ExternalInput")
    sdb_in = DT("sdb", [1, H], F32, kind="ExternalInput")
    g1r_in = DT("g1r", [32, 4, 32], F32, kind="ExternalInput")
    g2r_in = DT("g2r", [4, 16, 16], F32, kind="ExternalInput")
    g1i_in = DT("g1i", [32, 4, 32], F32, kind="ExternalInput")
    g2i_in = DT("g2i", [4, 16, 16], F32, kind="ExternalInput")
    fbr_in = DT("fbr", [1, 512], F32, kind="ExternalInput")
    fbi_in = DT("fbi", [1, 512], F32, kind="ExternalInput")
    sel32_in = DT("sel32c", [32, 128], BF, kind="ExternalInput")
    msk32_in = DT("msk32c", [32, 16], F32, kind="ExternalInput")
    mvec_in = DT("mvecc", [128, 8], F32, kind="ExternalInput")
    selD_in = DT("selDc", [32, 512], BF, kind="ExternalInput")
    selG_in = DT("selGc", [16, 128], BF, kind="ExternalInput")
    ident_in = DT("identc", [128, 128], BF, kind="ExternalInput")

    out_d = DT("out", [NTOK, DM], F32, kind="ExternalOutput")
    wqbf_d = DT("wq_bf", [8, 128, DM], BF, kind="Internal")
    xnfm_d = DT("xnfm", [NCHUNK, 128, DM], BF, kind="Internal")
    snap_d = DT("snapd", [NCHUNK, 128, DM], BF, kind="Internal")
    cc_in_d = DT("cc_in", [128, 1024], BF, kind="Internal")
    cc_out_d = DT("cc_out", [NCORE * 128, 1024], BF, kind="Internal")

    tap_outs = {}
    es = ExitStack()
    P = Prog(nc, es)
    T = Track()
    for d in ("x", "w", "tp", "tq", "qw", "st", "gp", "xc"):
        P.add_dma_sem(d)

    SB = lambda n, s, dt: nc.alloc_sbuf_tensor(n, s, dt)
    snapR = [SB(f"snapR{i}", [128, 1024], BF) for i in range(2)]
    qfm = SB("qfm", [128, 8 * NTOK], BF)
    AB = SB("AB", [128, NCHUNK * 32], BF)
    Dall = SB("Dall", [128, NCHUNK * 16], BF)
    ones_col = SB("ones_col", [1, 128], BF)
    ones_mov = SB("ones_mov", [128, 1], BF)
    sel32 = SB("sel32", [32, 128], BF)
    msk32 = SB("msk32", [32, 16], F32)
    smbias = SB("smbias", [1, NSM], BF)
    d32 = SB("d32", [32, 4], F32)
    d32m = SB("d32m", [32, 32], BF)
    tiny = SB("tiny", [128, 16], F32)
    bnb = SB("bnb", [128, 12], F32)
    psich = SB("psich", [128, 256], F32)
    pbf = SB("pbf", [128, 16], BF)
    smf2 = [SB(f"smf{i}", [128, NSM], F32) for i in range(2)]
    smb2 = [SB(f"smb{i}", [128, 16], BF) for i in range(2)]
    smg2 = [SB(f"smg{i}", [128, NSM], F32) for i in range(3)]
    cbias = SB("cbias", [128, 2], F32)
    ident = SB("ident", [128, 128], BF)

    base = (nc.sbuf_top - nc.sbuf_bytes_remaining + 31) & ~31
    ov = [base]

    def place(name, shape, dtype):
        t = nc.alloc_sbuf_tensor_at(name, shape, dtype, offset=ov[0])
        sz = int(np.prod(shape[1:])) * mybir.dt.size(dtype)
        ov[0] += (sz + 31) & ~31
        return t

    a0 = ov[0]
    xbuf = [place(f"xbuf{i}", [128, DM], F32) for i in range(2)]
    xn_tm2 = [place(f"xn_tm{i}", [128, DM], BF) for i in range(2)]
    xnsl = [place(f"xnsl{i}", [128, 8 * STOK], BF) for i in range(2)]
    kk2 = [place(f"kk2_{i}", [128, 2048], BF) for i in range(3)]
    vbt = [place(f"vbt{i}", [128, 1024], BF) for i in range(3)]
    psj3 = [place(f"psj{i}", [128, 1024], F32) for i in range(3)]
    wqb = [place(f"wqb{i}", [128, DM], BF) for i in range(2)]
    wmov = place("wmov", [128, 8 * 2128], BF)
    lam2b = [place(f"lam2b{i}", [128, 32], BF) for i in range(2)]
    assert ov[0] <= nc.sbuf_top, f"A overlay over by {ov[0] - nc.sbuf_top}B"

    ov[0] = a0
    wo_b = place("wo_b", [128, 8 * DM], BF)
    wffn = place("wffn", [128, 8 * DM], BF)
    corr = [place(f"corr{i}", [128, 1024], BF) for i in range(2)]
    snapt = place("snapt", [128, 1024], BF)
    fbias = place("fbias", [1, 1024], BF)
    dloop0 = ov[0]
    xbufD = [place(f"xbufD{i}", [128, DM], F32) for i in range(2)]
    xnfmD = [place(f"xnfmD{i}", [128, DM], BF) for i in range(2)]
    snapD = [place(f"snapD{i}", [128, DM], BF) for i in range(2)]
    o_comb = [place(f"o_comb{i}", [128, 1024], BF) for i in range(2)]
    gbuf = [place(f"gbuf{i}", [128, 1024], BF) for i in range(2)]
    outb = [place(f"outb{i}", [128, DM], F32) for i in range(2)]
    o_fm = place("o_fm", [128, 1024], BF)
    assert ov[0] <= nc.sbuf_top, f"D overlay over by {ov[0] - nc.sbuf_top}B"
    ov[0] = dloop0
    wstageD = place("wstageD", [128, 1024], F32)
    fwrt = place("fwrt", [128, 4 * 512], F32)
    fwit = place("fwit", [128, 4 * 512], F32)
    g1s = place("g1s", [128, 32], F32)
    g2s = place("g2s", [128, 16], F32)
    fttmp = place("fttmp", [128, 512], F32)
    selD = place("selD", [32, 512], BF)
    selG = place("selG", [16, 128], BF)
    mvec = place("mvec", [128, 8], F32)
    g1f = place("g1f", [32, 32], F32)
    g2f = place("g2f", [16, 16], F32)
    g1t = place("g1t", [32, 32], BF)
    g2t = place("g2t", [16, 16], BF)
    gath = [place("gath0", [128, 1024], BF)]
    state_in = place("state_in", [128, 1024], BF)
    assert ov[0] <= nc.sbuf_top, f"D2 overlay over by {ov[0] - nc.sbuf_top}B"

    kps = nc.alloc_psum_tensor("kps", [128, 1024], F32)
    vps = nc.alloc_psum_tensor("vps", [128, 1024], F32)
    smps = nc.alloc_psum_tensor("smps", [128, 512], F32)
    Sps = nc.alloc_psum_tensor("Sps", [128, 1024], F32)
    mps = nc.alloc_psum_tensor("mps", [128, 512], F32)

    tap_n = [0]

    def tap(name, src_name, src_ap):
        if name not in taps:
            return
        names = [src_name] if isinstance(src_name, str) else list(src_name)
        shp = list(src_ap.shape)
        t = DT("tap_" + name, shp, src_ap.dtype, kind="ExternalOutput")
        tap_outs[name] = t
        idx = tuple(slice(0, s) for s in shp)
        tap_n[0] += 1
        waits = []
        for n in names:
            waits += T.rw(n)
        tok = P.dma("sync", "st", t[idx], src_ap, rot=tap_n[0],
                    waits=[w for w in waits if w])
        for n in names:
            T.did_read(n, tok)

    def V(meth, out_name, out_ap, ins, *args, eng="vector", extra_waits=(),
          also_read=(), **kw):
        waits = list(extra_waits) + T.ww(out_name)
        for n, _ in ins:
            waits += T.rw(n)
        for n in also_read:
            waits += T.rw(n)
        m = meth
        if eng == "scalar" and meth == "tensor_copy":
            m = "copy"
        tok = P.op(eng, m, out_ap, *[a for _, a in ins], *args,
                   waits=waits, **kw)
        for n, _ in ins:
            T.did_read(n, tok)
        for n in also_read:
            T.did_read(n, tok)
        T.did_write(out_name, tok)
        return tok

    def DMA(eng, q, dst_name, dst_ap, src_name, src_ap, extra_waits=(),
            rot=0, fresh=True, **kw):
        waits = list(extra_waits) + T.ww(dst_name)
        if src_name:
            waits += T.rw(src_name)
        tok = P.dma(eng, q, dst_ap, src_ap, rot=rot, fresh=fresh,
                    waits=waits, **kw)
        if src_name:
            T.did_read(src_name, tok)
        T.did_write(dst_name, tok)
        return tok

    def mm_group(mms, reads, writes, extra_waits=()):
        waits = list(extra_waits)
        for n in reads:
            waits += T.rw(n)
        for n in writes:
            waits += T.ww(n)
        tok = None
        for i, (o, l, r, st, sp, tp) in enumerate(mms):
            kw = {"start": st, "stop": sp}
            if tp is not None:
                kw["tile_position"] = tp
            tok = P.op("tensor", "matmul", o, l, r, inc=(i == len(mms) - 1),
                       waits=waits if i == 0 else (), **kw)
        for n in reads:
            T.did_read(n, tok)
        for n in writes:
            T.did_write(n, tok)
        return tok

    t = P.op("gpsimd", "memset", ones_col[:, :], 1.0)
    T.did_write("ones_col", t)
    t = P.op("gpsimd", "memset", ones_mov[:, :], 1.0)
    T.did_write("ones_mov", t)
    t = P.op("gpsimd", "memset", cbias[:, 0:1], -2.5)
    T.did_write("cbias", t)
    t = P.op("gpsimd", "memset", cbias[:, 1:2], 0.5)
    T.did_write("cbias", t)
    t = P.op("gpsimd", "memset", psich[:, :], 0.0)
    for _n in ("psi_rk0", "psi_rv0", "psi_dot0", "psi_rk1", "psi_rv1",
               "psi_dot1", "p_den", "p_rden", "p_scr",
               "p_psi", "p_sur", "p_t2", "p_r2", "psi_rkv0", "psi_rkv1",
               "p_f1", "p_f2", "p_f3", "p_f4"):
        T.did_write(_n, t)
    wrot = [0]

    def wnext():
        wrot[0] += 1
        return wrot[0]

    for nm, sb, src in (("sel32", sel32, sel32_in), ("msk32", msk32, msk32_in),
                        ("ident", ident, ident_in)):
        DMA("sync", "w", nm, sb[:, :], None, src[:, :], rot=wnext())

    t = P.op("gpsimd", "memset", smbias[0:1, :], 0.0)
    T.did_write("smbias", t)
    DMA("sync", "w", "tfd", tiny[0:1, 0:16], None, fdb_in[:, :], rot=wnext())
    DMA("sync", "w", "smf0", smf2[0][0:1, 0:16], None, sdb_in[:, :], rot=wnext())
    tok = V("tensor_copy", "smbias", smbias[0:1, 16:32], [("tfd", tiny[0:1, 0:16])])
    tok = V("tensor_copy", "smbias", smbias[0:1, 32:48], [("smf0", smf2[0][0:1, 0:16])])

    for kc in range(8):
        b = kc * 2128
        DMA("sync", "w", "xbuf0", xbuf[0][:, :], None,
            w_in["wk"][kc * 128:(kc + 1) * 128, :], rot=wnext())
        V("tensor_copy", f"wmov_k{kc}", wmov[:, b:b + 1024],
          [("xbuf0", xbuf[0][:, :])])
        DMA("sync", "w", "xbuf0", xbuf[0][:, :], None,
            w_in["wv"][kc * 128:(kc + 1) * 128, :], rot=wnext())
        V("tensor_copy", f"wmov_v{kc}", wmov[:, b + 1024:b + 2048],
          [("xbuf0", xbuf[0][:, :])], eng="scalar")
        srot = wnext()
        for si, snm in enumerate(("wb", "wfd", "wsd", "wfg", "wsg")):
            DMA("sync", "w", "smf0", smf2[0][:, si * 16:(si + 1) * 16], None,
                sm_in[snm][kc * 128:(kc + 1) * 128, :], rot=srot,
                fresh=(si == 0))
        V("tensor_copy", f"wmov_s{kc}", wmov[:, b + 2048:b + 2128],
          [("smf0", smf2[0][:, :])])
    wmov_toks = [T.w[f"wmov_k{k}"] for k in range(8)] + \
        [T.w[f"wmov_v{k}"] for k in range(8)] + \
        [T.w[f"wmov_s{k}"] for k in range(8)]

    for kc in range(8):
        DMA("sync", "w", "xbuf0", xbuf[0][:, :], None,
            w_in["wq"][kc * 128:(kc + 1) * 128, :], rot=wnext())
        V("tensor_copy", "wqb0", wqb[0][:, :], [("xbuf0", xbuf[0][:, :])],
          eng="scalar")
        DMA("sync", "w", f"wqbf{kc}", wqbf_d[kc, :, :], "wqb0", wqb[0][:, :],
            rot=wnext())

    I32 = mybir.dt.int32
    MAGIC = 0x5f3759df

    def rsqrt_chain(a_name, a_ap, r_name, r_ap, t_name, t_ap, iters=2,
                    also=()):
        V("tensor_scalar_max", a_name, a_ap, [(a_name, a_ap)], 1e-30,
          also_read=also)
        V("tensor_single_scalar", r_name, r_ap.bitcast(I32),
          [(a_name, a_ap.bitcast(I32))], 1, ALU.arith_shift_right)
        V("tensor_scalar", r_name, r_ap.bitcast(I32),
          [(r_name, r_ap.bitcast(I32))], -1, MAGIC, ALU.mult, ALU.add)
        for _ in range(iters):
            V("tensor_mul", t_name, t_ap, [(r_name, r_ap), (r_name, r_ap)])
            V("tensor_mul", t_name, t_ap, [(t_name, t_ap), (a_name, a_ap)])
            V("tensor_scalar", t_name, t_ap, [(t_name, t_ap)], -0.5, 1.5,
              ALU.mult, ALU.add)
            V("tensor_mul", r_name, r_ap, [(r_name, r_ap), (t_name, t_ap)])

    smb_bc2 = [smb2[i][:, 0:16].unsqueeze(2).broadcast_to([128, 16, 64])
               for i in range(2)]
    pbf_bc = pbf[:, 0:16].unsqueeze(2).broadcast_to([128, 16, 64])

    C_RK, C_RV, C_DOT = 0, 16, 32
    C_S0, C_SRK = 48, 80
    C_DEN, C_RDEN, C_SCR, C_PSI, C_SUR = 112, 128, 144, 160, 176

    def pc(c0):
        return psich[:, c0:c0 + 16]

    def refined_sqrt(src_name, dst_name, dst_c):
        V("activation", "p_s0", pc(C_S0), [(src_name, pc({"psi_rk": C_RK,
          "psi_rv": C_RV}[src_name]))], AF.Sqrt, eng="scalar")
        V("reciprocal", "p_tmp", pc(C_TMP), [("p_s0", pc(C_S0))])
        V("tensor_mul", "p_tmp", pc(C_TMP),
          [(src_name, pc({"psi_rk": C_RK, "psi_rv": C_RV}[src_name])),
           ("p_tmp", pc(C_TMP))])
        V("tensor_add", "p_tmp", pc(C_TMP),
          [("p_s0", pc(C_S0)), ("p_tmp", pc(C_TMP))])
        V("tensor_scalar", dst_name, pc(dst_c), [("p_tmp", pc(C_TMP))],
          0.5, 1e-8, ALU.mult, ALU.add)

    kk2v = [kk2[i].ap().rearrange("p (h j d) -> p h j d", h=16, j=2)
            for i in range(3)]
    vbtv = [vbt[i].ap().rearrange("p (h d) -> p h d", h=16) for i in range(3)]
    junkv3 = [psj3[i].ap().rearrange("p (h d) -> p h d", h=16)
              for i in range(3)]

    def st_xload(c):
        DMA("sync", "x", f"xbuf{c % 2}", xbuf[c % 2][:, :], None,
            x_in[c * 128:(c + 1) * 128, :], rot=c)

    def st_lnxn(c):
        xname = f"xbuf{c % 2}"
        xb = xbuf[c % 2]
        xn_tm, xn_name = xn_tm2[c % 2], f"xn_tm{c % 2}"
        V("bn_stats", "bnb", bnb[:, 0:6], [(xname, xb[:, 0:512])])
        V("bn_stats", "bnb", bnb[:, 6:12], [(xname, xb[:, 512:1024])])
        V("bn_aggr", "t01", tiny[:, 0:2], [("bnb", bnb[:, 0:12])])
        V("tensor_scalar", "t2", tiny[:, 2:3], [("t01", tiny[:, 1:2])],
          1e-5, None, ALU.add)
        rsqrt_chain("t2", tiny[:, 2:3], "t1r", tiny[:, 1:2],
                    "t4", tiny[:, 4:5])
        V("tensor_scalar", xn_name, xn_tm[:, :], [(xname, xb[:, :])],
          tiny[:, 0:1], tiny[:, 1:2], ALU.subtract, ALU.mult,
          also_read=("t1r", "t01"))
        tap(f"xn{c}", xn_name, xn_tm[:, :])

    def st_xT(c):
        s4, cpos = c // SUP, c % SUP
        sl = xnsl[s4 % 2]
        slname = f"sl{s4 % 2}c{cpos}"
        xn_tm, xn_name = xn_tm2[c % 2], f"xn_tm{c % 2}"
        mms = []
        for kc in range(8):
            mms.append((Sps[:, kc * 128:(kc + 1) * 128],
                        xn_tm[:, kc * 128:(kc + 1) * 128],
                        ident[:, :], True, True, None))
        mm_group(mms, reads=[xn_name, "ident"], writes=["Sps"])
        V("tensor_copy", slname,
          bass.AP(sl, cpos * 128,
                  [[8 * STOK, 128], [STOK, 8], [1, 128]]),
          [("Sps", Sps.ap().rearrange("p (b t) -> p b t", b=8))],
          eng="vector")
        if "noxc" not in skip:
            DMA("sync", "xc", f"xnfmd{c}", xnfm_d[c, :, :], slname,
                bass.AP(sl, cpos * 128, [[8 * STOK, 128], [STOK, 8], [1, 128]]),
                rot=c)

    def st_kvsm(c):
        s4, cpos = c // SUP, c % SUP
        sl = xnsl[s4 % 2]
        slname = f"sl{s4 % 2}c{cpos}"
        mms = []
        for kc in range(8):
            st0, sp7 = (kc == 0), (kc == 7)
            lhs = sl[:, kc * STOK + cpos * 128: kc * STOK + cpos * 128 + 128]
            b = kc * 2128
            mms += [
                (kps[:, 0:512], lhs, wmov[:, b:b + 512], st0, sp7, None),
                (kps[:, 512:1024], lhs, wmov[:, b + 512:b + 1024], st0, sp7, None),
                (vps[:, 0:512], lhs, wmov[:, b + 1024:b + 1536], st0, sp7, None),
                (vps[:, 512:1024], lhs, wmov[:, b + 1536:b + 2048], st0, sp7, None),
                (smps[:, 0:NSM], lhs, wmov[:, b + 2048:b + 2128], st0, False, None),
            ]
        mms.append((smps[:, 0:NSM], ones_col[:, :], smbias[0:1, :],
                    False, True, None))
        mm_group(mms, reads=[slname, "wmov", "ones_col", "smbias"],
                 writes=["kps", "vps", "smps"],
                 extra_waits=wmov_toks if c == 0 else ())

    def st_epi(c):
        p = c % 3
        sp = c % 2
        sg = c % 3
        ka, vn = f"kk2a{p}", f"vbt{p}"
        smfn, smbn, smgn = f"smf{sp}", f"smb{sp}", f"smg{sg}"
        V("activation", smfn, smf2[sp][:, :], [("smps", smps[:, 0:NSM])],
          AF.Tanh, scale=0.5, eng="scalar")
        V("activation", ka, kk2v[p][:, :, 0, :], [("kps", kps[:, :])], AF.Silu,
          eng="scalar")
        V("activation", vn, vbtv[p], [("vps", vps[:, :])], AF.Silu,
          eng="scalar")
        V("activation", smbn, smb2[sp][:, :], [(smfn, smf2[sp][:, 0:16])],
          AF.Copy, bias=0.5, scale=0.5, eng="scalar")
        V("activation", smgn, smg2[sg][:, :], [(smfn, smf2[sp][:, :])],
          AF.Copy, bias=0.5, scale=0.5, eng="scalar")
        tap(f"smf{c}", smgn, smg2[sg][:, :])
        V("tensor_tensor", ka, kk2v[p][:, :, 0, :],
          [(ka, kk2v[p][:, :, 0, :]), (smbn, smb_bc2[sp])], ALU.mult,
          eng="gpsimd")
        V("tensor_tensor", vn, vbtv[p],
          [(vn, vbtv[p]), (smbn, smb_bc2[sp])], ALU.mult, eng="gpsimd")
        tap(f"vb{c}", vn, vbt[p][:, :])

    def st_psi1(c):
        p = c % 3
        rb = 0 if c % 2 == 0 else 192
        ka, vn = f"kk2a{p}", f"vbt{p}"
        kbv = kk2v[p][:, :, 0, :]
        V("activation", "psj0", junkv3[0], [(ka, kbv)], AF.Square,
          eng="scalar")
        V("activation", "psj1", junkv3[1], [(vn, vbtv[p])], AF.Square,
          eng="scalar")
        V("tensor_tensor", "psj2", junkv3[2], [(ka, kbv), (vn, vbtv[p])],
          ALU.mult, eng="gpsimd")
        rk = psich[:, rb + C_RK:rb + C_RK + 16]
        rv = psich[:, rb + C_RV:rb + C_RV + 16]
        dot = psich[:, rb + C_DOT:rb + C_DOT + 16]
        V("reduce_sum", f"psi_rk{c % 2}", rk, [("psj0", junkv3[0])], AXX,
          also_read=(f"psi_rkv{c % 2}",))
        V("reduce_sum", f"psi_rv{c % 2}", rv, [("psj1", junkv3[1])], AXX,
          also_read=(f"psi_rkv{c % 2}",))
        V("reduce_sum", f"psi_dot{c % 2}", dot, [("psj2", junkv3[2])], AXX)

    def st_psi2(c):
        p = c % 3
        sp = c % 2
        sg = c % 3
        rb = 0 if c % 2 == 0 else 192
        rk = psich[:, rb + C_RK:rb + C_RK + 16]
        rv = psich[:, rb + C_RV:rb + C_RV + 16]
        dot = psich[:, rb + C_DOT:rb + C_DOT + 16]
        ka = f"kk2a{p}"
        kbv = kk2v[p][:, :, 0, :]
        V("tensor_mul", "p_den", pc(C_DEN),
          [(f"psi_rk{sp}", rk), (f"psi_rv{sp}", rv)])
        rsqrt_chain("p_den", pc(C_DEN), "p_rden", pc(C_RDEN),
                    "p_t2", psich[:, C_SRK:C_SRK + 16])
        V("activation", "p_scr", pc(C_SCR), [(f"psi_dot{sp}", dot)], AF.Abs,
          eng="scalar")
        V("tensor_mul", "p_scr", pc(C_SCR), [("p_scr", pc(C_SCR)),
                                             ("p_rden", pc(C_RDEN))])
        V("activation", "p_psi", pc(C_PSI), [("p_scr", pc(C_SCR))], AF.Tanh,
          scale=1.5, eng="scalar")
        V("activation", "p_sur", pc(C_SUR), [("p_psi", pc(C_PSI))], AF.Tanh,
          scale=2.5, eng="scalar")
        V("activation", "pbf", pbf[:, :], [("p_sur", pc(C_SUR))], AF.Square,
          bias=cbias[:, 1:2], scale=0.5, eng="scalar", also_read=("cbias",))
        tap(f"psi{c}", (f"psi_rkv{sp}", f"psi_dot{sp}", "p_r2", "p_t2",
                        "p_den", "p_rden", "p_scr", "p_psi", "p_sur",
                        "p_f1", "p_f2", "p_f3", "p_f4"),
            psich[:, rb:rb + 192] if rb == 0 else psich[:, 0:192])
        V("tensor_tensor", f"kk2b{p}", kk2v[p][:, :, 1, :],
          [(ka, kbv), ("pbf", pbf_bc)], ALU.mult, eng="gpsimd")
        tap(f"kb{c}", (ka, f"kk2b{p}"), kk2[p][:, :])
        folds = [("p_f1", C_S0, -0.05, 0.95), ("p_f2", C_S0 + 16, -0.025,
                 0.975), ("p_f3", C_SRK, 0.15, 0.65), ("p_f4", C_SRK + 16,
                 -0.15, 0.35)]
        for fn, col, a, b in folds:
            V("activation", fn, psich[:, col:col + 16],
              [("p_psi", pc(C_PSI))], AF.Copy, bias=b, scale=a, eng="scalar")
        tok = V("tensor_mul", f"psi_rkv{sp}", rk,
                [("p_f1", psich[:, C_S0:C_S0 + 16]),
                 (f"smg{sg}", smg2[sg][:, 16:32])],
                also_read=(f"psi_rk{sp}",), eng="gpsimd")
        T.did_write(f"psi_rk{sp}", tok)
        tok = V("tensor_mul", f"psi_rkv{sp}", rv,
                [("p_f2", psich[:, C_S0 + 16:C_S0 + 32]),
                 (f"smg{sg}", smg2[sg][:, 32:48])],
                also_read=(f"psi_rv{sp}",), eng="gpsimd")
        T.did_write(f"psi_rv{sp}", tok)
        tok = V("tensor_copy", f"lam2b{sp}", lam2b[sp][:, :],
                [(f"psi_rkv{sp}", psich[:, rb + C_RK:rb + C_RK + 32])],
                eng="gpsimd")
        T.did_read(f"psi_rk{sp}", tok)
        T.did_read(f"psi_rv{sp}", tok)
        V("tensor_mul", f"ab{c}", AB[:, 32 * c:32 * c + 16],
          [("p_f3", psich[:, C_SRK:C_SRK + 16]),
           (f"smg{sg}", smg2[sg][:, 48:64])], eng="gpsimd")
        V("tensor_mul", f"ab{c}b", AB[:, 32 * c + 16:32 * c + 32],
          [("p_f4", psich[:, C_SRK + 16:C_SRK + 32]),
           (f"smg{sg}", smg2[sg][:, 64:80])], eng="gpsimd")

    def st_wdsm(c):
        p = c % 2
        mm_group([(mps[0:32, 0:1], lam2b[p][:, :], ones_mov[:, :], True, True,
                   None)],
                 reads=[f"lam2b{p}", "ones_mov"], writes=["mps"])
        b0 = 2 * p
        V("activation", "d32", d32[:, b0:b0 + 1], [("mps", mps[0:32, 0:1])],
          AF.Copy, bias=0.0, scale=1.0 / 128.0, eng="scalar")
        for i in range(7):
            a, bcol = b0 + i % 2, b0 + (i + 1) % 2
            V("activation", "d32", d32[:, bcol:bcol + 1],
              [("d32", d32[:, a:a + 1])], AF.Square, eng="scalar")
        dcol = b0 + 1
        dm = d32m[:, 16 * p:16 * p + 16]
        V("tensor_scalar", f"d32m{p}", dm, [("msk32", msk32[:, :])],
          d32[:, dcol:dcol + 1], None, ALU.mult, also_read=("d32",))
        mm_group([(mps[:, 16:32], sel32[:, :], dm, True, True, None)],
                 reads=["sel32", f"d32m{p}"], writes=["mps2"])
        V("tensor_copy", f"dall{c}", Dall[:, 16 * c:16 * c + 16],
          [("mps2", mps[:, 16:32])], eng="scalar")
        tap(f"dsm{c}", f"dall{c}", Dall[:, 16 * c:16 * c + 16])

    def st_S(c):
        p = c % 3
        mms = []
        for j in range(8):
            h1, h2 = 2 * j, 2 * j + 1
            mms.append((Sps[0:64, j * 128:(j + 1) * 128],
                        vbt[p][:, h1 * 64:(h1 + 1) * 64],
                        kk2[p][:, h1 * 128:(h1 + 1) * 128], True, True, (0, 0)))
            mms.append((Sps[64:128, j * 128:(j + 1) * 128],
                        vbt[p][:, h2 * 64:(h2 + 1) * 64],
                        kk2[p][:, h2 * 128:(h2 + 1) * 128], True, True, (0, 64)))
        mm_group(mms, reads=[f"vbt{p}", f"kk2a{p}", f"kk2b{p}"], writes=["Sps"])

    def st_chain(c):
        sn = f"snapR{c % 2}"
        snap_c = snapR[c % 2][:, :]
        if c == 0:
            V("tensor_copy", sn, snap_c, [("Sps", Sps[:, :])])
        else:
            dview = Dall[:, 16 * c:16 * c + 16] \
                .rearrange("p (j s) -> p j s", j=8) \
                .unsqueeze(3).broadcast_to([128, 8, 2, 64])
            prev = snapR[(c - 1) % 2].ap() \
                .rearrange("p (j s e) -> p j s e", j=8, s=2)
            V("tensor_tensor", sn,
              snap_c.rearrange("p (j s e) -> p j s e", j=8, s=2),
              [(f"snapR{(c - 1) % 2}", prev), (f"dall{c}", dview)], ALU.mult,
              eng="gpsimd")
            V("tensor_add", sn, snap_c, [(sn, snap_c), ("Sps", Sps[:, :])])
        tap(f"snap{c}", sn, snap_c)

    def st_snapst(c):
        DMA("sync", "tp", f"snapd{c}", snap_d[c, :, :], f"snapR{c % 2}",
            snapR[c % 2][:, :], rot=c)

    def st_q(s):
        banks = [kps[:, 0:512], kps[:, 512:1024], vps[:, 0:512],
                 vps[:, 512:1024], smps[:, 0:512], Sps[:, 0:512],
                 Sps[:, 512:1024], mps[:, 0:512]]
        bank_names = ["kps", "kps", "vps", "vps", "smps", "Sps", "Sps", "mps"]
        bank_all = ["kps", "vps", "smps", "Sps", "mps", "mps2"]
        sl = xnsl[s % 2]
        slnames = [f"sl{s % 2}c{i}" for i in range(SUP)]
        for kc in range(8):
            if "qnw" not in skip or kc < 2:
                DMA("gpsimd", "qw", f"wqb{kc % 2}", wqb[kc % 2][:, :],
                    f"wqbf{kc}", wqbf_d[kc, :, :], rot=kc)
            mms = []
            for m in range(8):
                mov = wmov[:, 0:STOK] if "qdummy" in skip \
                    else sl[:, kc * STOK:(kc + 1) * STOK]
                mms.append((banks[m], wqb[kc % 2][:, m * 128:(m + 1) * 128],
                            mov, kc == 0, kc == 7, None))
            rds = (["wmov"] if "qdummy" in skip else slnames) + \
                [f"wqb{kc % 2}"]
            grp = mm_group(mms, reads=rds,
                           writes=(bank_all if kc == 0 else []),
                           extra_waits=([T.w.get("qgrp")] if kc > 0 else ()))
            T.did_write("qgrp", grp)
            if kc > 0:
                for n in bank_all:
                    T.did_write(n, grp)
        if "qnosil" in skip:
            return
        for m in range(8):
            V("activation", f"qfm{m}",
              qfm[:, m * NTOK + s * STOK: m * NTOK + (s + 1) * STOK],
              [(bank_names[m], banks[m])], AF.Silu, eng="scalar")
        tap(f"q{s}", "qfm0",
            qfm[:, 0 * NTOK + s * STOK: 0 * NTOK + (s + 1) * STOK])

    st_xload(0)
    st_lnxn(0)
    st_xT(0)
    st_xload(1)
    st_lnxn(1)
    st_xload(2)
    for c in range(NCHUNK + 2):
        if c + 3 < NCHUNK:
            st_xload(c + 3)
        if c + 2 < NCHUNK:
            st_lnxn(c + 2)
        if c >= 3:
            st_snapst(c - 3)
        if 0 <= c - 2 < NCHUNK:
            st_psi2(c - 2)
        if c < NCHUNK:
            st_kvsm(c)
            st_epi(c)
        if c + 1 < NCHUNK:
            st_xT(c + 1)
        if c < NCHUNK:
            st_psi1(c)
        if c >= 2:
            st_wdsm(c - 2)
            st_S(c - 2)
            st_chain(c - 2)
        if c >= 1 and (c - 1) % SUP == SUP - 1 and "q" not in skip:
            st_q((c - 1) // SUP)
    st_snapst(NCHUNK - 1)

    if use_cc:
        DMA("gpsimd", "gp", "cc_in", cc_in_d[:, :], f"snapR{(NCHUNK - 1) % 2}",
            snapR[(NCHUNK - 1) % 2][:, :], rot=0)
        t_cc = P.op("gpsimd", "collective_compute", "AllGather", ALU.bypass,
                    [[i for i in range(NCORE)]],
                    [cc_in_d.ap().opt()], [cc_out_d.ap().opt()],
                    waits=T.rw("cc_in") + [T.w.get("cc_in")])
        T.did_write("cc_out", t_cc)

    P.fence_all()

    for nm, sb, src in (("selD", selD, selD_in), ("selG", selG, selG_in),
                        ("mvec", mvec, mvec_in)):
        DMA("sync", "w", nm, sb[:, :], None, src[:, :], rot=wnext())
    DMA("sync", "w", "wsD", wstageD[0:1, 0:512], None, fbr_in[:, :],
        rot=wnext())
    V("tensor_copy", "fbias", fbias[0:1, 0:512], [("wsD", wstageD[0:1, 0:512])])
    DMA("sync", "w", "wsD", wstageD[0:1, 0:512], None, fbi_in[:, :],
        rot=wnext())
    V("tensor_copy", "fbias2", fbias[0:1, 512:1024],
      [("wsD", wstageD[0:1, 0:512])])

    for kc in range(8):
        DMA("sync", "w", "wsD", wstageD[:, :], None,
            w_in["wo"][kc * 128:(kc + 1) * 128, :], rot=wnext())
        V("tensor_copy", f"wo{kc}", wo_b[:, kc * 1024:(kc + 1) * 1024],
          [("wsD", wstageD[:, :])], eng="scalar")
    wo_toks = [T.w[f"wo{k}"] for k in range(8)]

    for which, g1src, g2src, dest, dname in (
            ("r", g1r_in, g2r_in, fwrt, "fwrt"), ("i", g1i_in, g2i_in, fwit,
                                                  "fwit")):
        for c4 in range(4):
            DMA("sync", "w", "g1f", g1f[:, :], None,
                bass.AP(g1src, c4 * 32, [[1, 32], [128, 32]]),
                allow_slow_non_contiguous=True, rot=wnext())
            DMA("sync", "w", "g2f", g2f[:, :], None,
                bass.AP(g2src, c4 * 256, [[1, 16], [16, 16]]),
                allow_slow_non_contiguous=True, rot=wnext())
            V("tensor_copy", "g1t", g1t[:, :], [("g1f", g1f[:, :])])
            V("tensor_copy", "g2t", g2t[:, :], [("g2f", g2f[:, :])])
            mm_group([(mps[:, 32:48], selG[:, :], g2t[:, :], True, True, None)],
                     reads=["selG", "g2t"], writes=["mpsB"])
            V("tensor_copy", "g2s", g2s[:, :], [("mpsB", mps[:, 32:48])])
            for t4 in range(4):
                mm_group([(mps[:, 0:32], selD[:, t4 * 128:(t4 + 1) * 128],
                           g1t[:, :], True, True, None)],
                         reads=["selD", "g1t"], writes=["mpsA"])
                V("tensor_copy", "g1s", g1s[:, :], [("mpsA", mps[:, 0:32])])
                a_bc = g1s.ap().unsqueeze(2).broadcast_to([128, 32, 16])
                b_bc = g2s.ap().unsqueeze(1).broadcast_to([128, 32, 16])
                dst = dest[:, t4 * 512:(t4 + 1) * 512] \
                    .rearrange("p (b e) -> p b e", b=32)
                if c4 == 0:
                    V("tensor_tensor", f"{dname}{t4}", dst,
                      [("g1s", a_bc), ("g2s", b_bc)], ALU.mult)
                else:
                    V("tensor_tensor", "fttmp",
                      fttmp.ap().rearrange("p (b e) -> p b e", b=32),
                      [("g1s", a_bc), ("g2s", b_bc)], ALU.mult)
                    V("tensor_add", f"{dname}{t4}", dst,
                      [(f"{dname}{t4}", dst), ("fttmp", fttmp.ap()
                        .rearrange("p (b e) -> p b e", b=32))])
    for kc in range(8):
        b = kc * 1024
        if kc < 4:
            V("tensor_copy", f"wffn{kc}a", wffn[:, b:b + 512],
              [(f"fwrt{kc}", fwrt[:, kc * 512:(kc + 1) * 512])])
            V("tensor_copy", f"wffn{kc}b", wffn[:, b + 512:b + 1024],
              [(f"fwit{kc}", fwit[:, kc * 512:(kc + 1) * 512])], eng="scalar")
        else:
            V("tensor_scalar", f"wffn{kc}a", wffn[:, b:b + 512],
              [(f"fwit{kc - 4}", fwit[:, (kc - 4) * 512:(kc - 3) * 512])],
              -1.0, None, ALU.mult)
            V("tensor_copy", f"wffn{kc}b", wffn[:, b + 512:b + 1024],
              [(f"fwrt{kc - 4}", fwrt[:, (kc - 4) * 512:(kc - 3) * 512])],
              eng="scalar")
    wffn_toks = [T.w[f"wffn{k}a"] for k in range(8)] + \
        [T.w[f"wffn{k}b"] for k in range(8)]

    if not use_cc:
        t = P.op("gpsimd", "memset", state_in[:, :], 0.0)
        T.did_write("state_in", t)
    for j in range(NCORE if use_cc else 0):
        DMA("gpsimd", "gp", "gath0", gath[0][:, :], "cc_out",
            cc_out_d[j * 128:(j + 1) * 128, :], rot=j)
        pp = [corr[0], state_in][j % 2]
        ppn = ["corr0", "state_in"][j % 2]
        prevn = ["corr0", "state_in"][(j + 1) % 2]
        if j == 0:
            V("tensor_scalar", ppn, pp[:, :], [("gath0", gath[0][:, :])],
              mvec[:, 0:1], None, ALU.mult, also_read=("mvec",))
        else:
            V("scalar_tensor_tensor", ppn, pp[:, :],
              [("gath0", gath[0][:, :])],
              mvec[:, j:j + 1],
              [corr[0], state_in][(j + 1) % 2][:, :],
              ALU.mult, ALU.add, also_read=(prevn, "mvec"))
    tap("state_in", "state_in", state_in[:, :])
    P.fence_all()

    kps8 = kps.ap().rearrange("p (h j d) -> p h j d", h=8, j=2)
    vps8 = vps.ap().rearrange("p (h j d) -> p h j d", h=8, j=2)

    def st_xD(c):
        DMA("sync", "x", f"xbufD{c % 2}", xbufD[c % 2][:, :], None,
            x_in[c * 128:(c + 1) * 128, :], rot=c)

    def st_xnfm(c):
        DMA("sync", "xc", f"xnfmD{c % 2}", xnfmD[c % 2][:, :], f"xnfmd{c}",
            xnfm_d[c, :, :], rot=c)

    def st_snapld(c):
        DMA("sync", "tq", f"snapD{c % 2}", snapD[c % 2][:, :], f"snapd{c}",
            snap_d[c, :, :], rot=c)

    def st_corr(c):
        dview = Dall[:, 16 * c:16 * c + 16] \
            .rearrange("p (j s) -> p j s", j=8) \
            .unsqueeze(3).broadcast_to([128, 8, 2, 64])
        cb, cbn = corr[c % 2], f"corr{c % 2}"
        pv, pvn = (state_in, "state_in") if c == 0 else \
            (corr[(c + 1) % 2], f"corr{(c + 1) % 2}")
        tok = V("tensor_tensor", cbn,
                cb.ap().rearrange("p (j s e) -> p j s e", j=8, s=2),
                [(pvn, pv.ap().rearrange("p (j s e) -> p j s e", j=8, s=2)),
                 (f"dall{c}", dview)], ALU.mult)
        if c == 0:
            T.did_read("outb0", tok)
        V("tensor_add", "snapt", snapt[:, :],
          [(f"snapD{c % 2}", snapD[c % 2][:, :]), (cbn, cb[:, :])])

    def st_ffn(c):
        p = c % 2
        mms = []
        for kc in range(8):
            st0 = kc == 0
            lhs = xnfmD[p][:, kc * 128:(kc + 1) * 128]
            b = kc * 1024
            mms.append((smps[:, 0:512], lhs, wffn[:, b:b + 512], st0, False,
                        None))
            mms.append((mps[:, 0:512], lhs, wffn[:, b + 512:b + 1024], st0,
                        False, None))
        mms.append((smps[:, 0:512], ones_col[:, :], fbias[0:1, 0:512], False,
                    True, None))
        mms.append((mps[:, 0:512], ones_col[:, :], fbias[0:1, 512:1024], False,
                    True, None))
        mm_group(mms, reads=[f"xnfmD{p}", "ones_col", "fbias", "fbias2"],
                 writes=["smps", "mps", "mpsA", "mpsB"],
                 extra_waits=(wffn_toks if c == 0 else ()))
        V("activation", f"gbuf{p}", gbuf[p][:, 0:512],
          [("smps", smps[:, 0:512])], AF.Silu, eng="scalar")
        V("activation", f"gbuf{p}", gbuf[p][:, 512:1024],
          [("mps", mps[:, 0:512])], AF.Silu, eng="scalar")
        tap(f"g{c}", f"gbuf{p}", gbuf[p][:, :])

    def st_o(c):
        mms = []
        for j in range(8):
            for hh in range(2):
                rows = slice(hh * 64, hh * 64 + 64)
                outap = [kps, vps][hh][:, j * 128:(j + 1) * 128]
                mms.append((outap,
                            qfm[rows, j * NTOK + c * 128: j * NTOK + (c + 1) * 128],
                            snapt[rows, j * 128:(j + 1) * 128],
                            True, True, None))
        mm_group(mms, reads=["snapt"] + [f"qfm{j}" for j in range(8)],
                 writes=["kps", "vps"])

    def st_gate(c):
        p = c % 2
        on = f"o_comb{p}"
        for par, psv, psn in ((0, kps8, "kps"), (1, vps8, "vps")):
            a_bc = bass.AP(AB, 32 * c + par,
                           [[AB.shape[1], 128], [2, 8], [0, 64]])
            b_bc = bass.AP(AB, 32 * c + 16 + par,
                           [[AB.shape[1], 128], [2, 8], [0, 64]])
            oc = bass.AP(o_comb[p], par * 64,
                         [[o_comb[p].shape[1], 128], [128, 8], [1, 64]])
            tmp = bass.AP(o_fm, par * 64,
                          [[o_fm.shape[1], 128], [128, 8], [1, 64]])
            V("tensor_tensor", on, oc,
              [(psn, psv[:, :, 0, :]), (f"ab{c}", a_bc)], ALU.mult)
            V("tensor_tensor", "o_fm", tmp,
              [(psn, psv[:, :, 1, :]), (f"ab{c}b", b_bc)], ALU.mult)
            V("tensor_add", on, oc, [(on, oc), ("o_fm", tmp)])
        tap(f"oc{c}", on, o_comb[p][:, :])

    def st_oT(c):
        p = c % 2
        mms = []
        for kc in range(8):
            mms.append((kps[:, kc * 128:(kc + 1) * 128],
                        o_comb[p][:, kc * 128:(kc + 1) * 128],
                        ident[:, :], True, True, None))
        mm_group(mms, reads=[f"o_comb{p}", "ident"], writes=["kps"])
        V("tensor_copy", "o_fm", o_fm[:, 0:512], [("kps", kps[:, 0:512])],
          eng="scalar")
        V("tensor_copy", "o_fm", o_fm[:, 512:1024], [("kps", kps[:, 512:1024])],
          eng="scalar")

    def st_wo(c):
        mms = []
        for kc in range(8):
            st0, sp7 = kc == 0, kc == 7
            lhs = o_fm[:, kc * 128:(kc + 1) * 128]
            b = kc * 1024
            mms.append((Sps[:, 0:512], lhs, wo_b[:, b:b + 512], st0, sp7, None))
            mms.append((Sps[:, 512:1024], lhs, wo_b[:, b + 512:b + 1024], st0,
                        sp7, None))
        mm_group(mms, reads=["o_fm"] + ([f"wo{k}" for k in range(8)]
                                        if c == 0 else []),
                 writes=["Sps"])

    def st_final(c):
        p = c % 2
        ob, obn = outb[p], f"outb{p}"
        V("tensor_tensor", obn, ob[:, :],
          [(f"gbuf{p}", gbuf[p][:, :]), ("Sps", Sps[:, :])], ALU.mult)
        V("tensor_add", obn, ob[:, :], [(obn, ob[:, :]),
                                        (f"xbufD{p}", xbufD[p][:, :])])
        DMA("sync", "st", f"out{c}", out_d[c * 128:(c + 1) * 128, :], obn,
            ob[:, :], rot=c)

    st_snapld(0)
    st_xnfm(0)
    st_corr(0)
    for c in range(0 if "d" not in skip else NCHUNK + 2, NCHUNK + 2):
        if c + 1 < NCHUNK:
            st_snapld(c + 1)
        if 0 <= c - 1 < NCHUNK:
            st_xD(c - 1)
        if c + 1 < NCHUNK:
            st_xnfm(c + 1)
        if c - 2 >= 0:
            st_final(c - 2)
        if c < NCHUNK:
            st_ffn(c)
        if c - 1 >= 0 and c - 1 < NCHUNK:
            st_oT(c - 1)
            st_wo(c - 1)
        if c < NCHUNK:
            st_o(c)
            st_gate(c)
            if c + 1 < NCHUNK:
                st_corr(c + 1)

    if "d" in skip:
        t = P.op("gpsimd", "memset", outb[0][:, :], 0.0)
        T.did_write("outb0", t)
        for c in range(NCHUNK):
            DMA("sync", "st", f"out{c}", out_d[c * 128:(c + 1) * 128, :],
                "outb0", out_d is not None and outb[0][:, :], rot=c, fresh=(c < 2))
    P.final_wait("sync", "st")
    P.final_wait("gpsimd", "gp")

    with nc.Block() as block, es:
        P.build(block)
    return nc, tap_outs



def make_consts(core):
    sel32 = np.zeros((32, 128), dtype=bf16)
    for k in range(32):
        h = k % 16
        par = h % 2
        sel32[k, par * 64:(par + 1) * 64] = 1.0
    sel32[:] = 0
    for k in range(32):
        h = k % 16
        for p in range(128):
            if h % 2 == p // 64:
                sel32[k, p] = 1.0
    msk32 = np.zeros((32, 16), dtype=np.float32)
    for k in range(32):
        s, h = k // 16, k % 16
        j = h // 2
        msk32[k, j * 2 + s] = 1.0
    mvec = np.zeros((128, 8), dtype=np.float32)
    if core % 2 == 1:
        mvec[:, core - 1] = 1.0
    selD = np.zeros((32, 512), dtype=bf16)
    for t4 in range(4):
        for p in range(128):
            selD[8 * t4 + p // 16, t4 * 128 + p] = 1.0
    selG = np.zeros((16, 128), dtype=bf16)
    for p in range(128):
        selG[p % 16, p] = 1.0
    return sel32, msk32, mvec, selD, selG


def core_inputs(core, inp):
    b, half = core // 2, core % 2
    x = np.ascontiguousarray(
        np.asarray(inp["L0_attn_hidden_states"])[b, half * NTOK:(half + 1) * NTOK, :])
    sel32, msk32, mvec, selD, selG = make_consts(core)
    asF = lambda a: np.ascontiguousarray(np.asarray(a, dtype=np.float32))
    return {
        "x": x.astype(np.float32),
        "wq": asF(inp["L0_attn_Wq"]), "wk": asF(inp["L0_attn_Wk"]),
        "wv": asF(inp["L0_attn_Wv"]), "wo": asF(inp["L0_attn_Wo"]),
        "wb": asF(inp["L0_attn_Wb"]), "wfd": asF(inp["L0_attn_Wfd"]),
        "wsd": asF(inp["L0_attn_Wsd"]), "wfg": asF(inp["L0_attn_Wfg"]),
        "wsg": asF(inp["L0_attn_Wsg"]),
        "fdb": asF(inp["L0_attn_fd_bias"]).reshape(1, H),
        "sdb": asF(inp["L0_attn_sd_bias"]).reshape(1, H),
        "g1r": asF(inp["L0_ffn_G1_R"]).reshape(32, 4, 32),
        "g2r": asF(inp["L0_ffn_G2_R"]).reshape(4, 16, 16),
        "g1i": asF(inp["L0_ffn_G1_I"]).reshape(32, 4, 32),
        "g2i": asF(inp["L0_ffn_G2_I"]).reshape(4, 16, 16),
        "fbr": asF(inp["L0_ffn_bias_real"]).reshape(1, 512),
        "fbi": asF(inp["L0_ffn_bias_imag"]).reshape(1, 512),
        "sel32c": sel32, "msk32c": msk32, "mvecc": mvec,
        "selDc": selD, "selGc": selG,
        "identc": np.eye(128, dtype=bf16),
    }


def kernel(**inputs):
    nc, _ = build_nc()
    in_maps = [core_inputs(i, inputs) for i in range(NCORE)]
    res = run_bass_kernel_spmd(nc, in_maps, core_ids=list(range(NCORE)))
    outs = [res.results[i]["out"] for i in range(NCORE)]
    full = np.empty((4, 8192, DM), dtype=np.float32)
    for i in range(NCORE):
        b, half = i // 2, i % 2
        full[b, half * NTOK:(half + 1) * NTOK, :] = outs[i]
    return full


# version 25
# speedup vs baseline: 1.0244x; 1.0244x over previous
import numpy as np
import ml_dtypes
import concourse.bass as bass
import concourse.mybir as mybir
from concourse.bass_utils import run_bass_kernel_spmd
from contextlib import ExitStack

BF = mybir.dt.bfloat16
F32 = mybir.dt.float32
AF = mybir.ActivationFunctionType
ALU = mybir.AluOpType
AXX = mybir.AxisListType.X

NCORE = 8
DM = 1024
H, HD = 16, 64
CHUNK = 128
NCHUNK = 32
NTOK = NCHUNK * CHUNK
SUP = 4
NSUP = NCHUNK // SUP
STOK = SUP * CHUNK
NSM = 80

bf16 = ml_dtypes.bfloat16
ENGS = ("sync", "scalar", "vector", "tensor", "gpsimd")


def set_dims(nchunk):
    global NCHUNK, NTOK, NSUP
    NCHUNK = nchunk
    NTOK = NCHUNK * CHUNK
    NSUP = NCHUNK // SUP
    assert NCHUNK % SUP == 0


class Prog:
    def __init__(self, nc, es):
        self.nc = nc
        self.sem = {e: es.enter_context(nc.semaphore(f"T{e}")) for e in ENGS}
        self.cnt = {e: 0 for e in ENGS}
        self.stream = {e: [] for e in ENGS}
        self.water = {e: {} for e in ENGS}
        self.dsem, self.dcnt = {}, {}
        self.es = es

    def add_dma_sem(self, name):
        for r in range(2):
            nm = f"{name}{r}"
            self.dsem[nm] = self.es.enter_context(self.nc.semaphore(f"D{nm}"))
            self.dcnt[nm] = 0

    def _wait(self, e, tok):
        key, sem, val = tok
        if self.water[e].get(key, 0) >= val:
            return
        self.water[e][key] = val
        self.stream[e].append(("w", sem, val))

    def _do_waits(self, e, waits):
        best = {}
        for t in waits:
            if t is None:
                continue
            k = t[0]
            if k not in best or t[2] > best[k][2]:
                best[k] = t
        for t in best.values():
            self._wait(e, t)

    def op(self, e, meth, *args, waits=(), inc=True, **kw):
        self._do_waits(e, waits)
        if inc:
            self.cnt[e] += 1
            self.stream[e].append(("i", meth, args, kw, self.sem[e], 1))
        else:
            self.stream[e].append(("o", meth, args, kw))
        return ("E" + e, self.sem[e], self.cnt[e])

    def dma(self, e, dname, out, in_, rot=0, fresh=True, waits=(), **kw):
        nm = f"{dname}{rot % 2}"
        s = self.dsem[nm]
        if fresh and self.dcnt[nm] > 0:
            self._wait(e, ("D" + nm, s, self.dcnt[nm]))
        self._do_waits(e, waits)
        self.dcnt[nm] += 16
        self.stream[e].append(("i", "dma_start", (), dict(out=out, in_=in_, **kw),
                               s, 16))
        return ("D" + nm, s, self.dcnt[nm])

    def tok(self, e):
        return ("E" + e, self.sem[e], self.cnt[e])

    def fence_all(self):
        toks = [self.tok(e) for e in ENGS] + \
            [("D" + d, self.dsem[d], self.dcnt[d]) for d in self.dsem]
        for e in ENGS:
            for t in toks:
                if t[2] > 0 and t[0] != "E" + e:
                    self._wait(e, t)

    def final_wait(self, e, dname):
        for r in range(2):
            nm = f"{dname}{r}"
            if self.dcnt[nm] > 0:
                self.stream[e].append(("w", self.dsem[nm], self.dcnt[nm]))

    def build(self, block):
        for e in ENGS:
            ops = self.stream[e]

            def runner(eng, ops=ops):
                for it in ops:
                    if it[0] == "w":
                        eng.wait_ge(it[1], it[2])
                    elif it[0] == "o":
                        getattr(eng, it[1])(*it[2], **it[3])
                    else:
                        getattr(eng, it[1])(*it[2], **it[3]).then_inc(it[4], it[5])
            getattr(block, e)(runner)


class Track:
    def __init__(self):
        self.w = {}
        self.r = {}

    def rw(self, name):
        return [self.w.get(name)]

    def ww(self, name):
        return self.r.get(name, []) + [self.w.get(name)]

    def did_read(self, name, tok):
        self.r.setdefault(name, []).append(tok)

    def did_write(self, name, tok):
        self.w[name] = tok
        self.r[name] = []


def build_nc(taps=(), use_cc=True, skip=()):
    skip = set(skip)
    taps = set(taps)
    nc = bass.Bass("TRN2", target_bir_lowering=False, debug=False)
    DT = nc.dram_tensor

    x_in = DT("x", [NTOK, DM], F32, kind="ExternalInput")
    w_in = {n: DT(n, [DM, DM], F32, kind="ExternalInput")
            for n in ("wq", "wk", "wv", "wo")}
    sm_in = {n: DT(n, [DM, H], F32, kind="ExternalInput")
             for n in ("wb", "wfd", "wsd", "wfg", "wsg")}
    fdb_in = DT("fdb", [1, H], F32, kind="ExternalInput")
    sdb_in = DT("sdb", [1, H], F32, kind="ExternalInput")
    g1r_in = DT("g1r", [32, 4, 32], F32, kind="ExternalInput")
    g2r_in = DT("g2r", [4, 16, 16], F32, kind="ExternalInput")
    g1i_in = DT("g1i", [32, 4, 32], F32, kind="ExternalInput")
    g2i_in = DT("g2i", [4, 16, 16], F32, kind="ExternalInput")
    fbr_in = DT("fbr", [1, 512], F32, kind="ExternalInput")
    fbi_in = DT("fbi", [1, 512], F32, kind="ExternalInput")
    sel32_in = DT("sel32c", [32, 128], BF, kind="ExternalInput")
    msk32_in = DT("msk32c", [32, 16], F32, kind="ExternalInput")
    mvec_in = DT("mvecc", [128, 8], F32, kind="ExternalInput")
    selD_in = DT("selDc", [32, 512], BF, kind="ExternalInput")
    selG_in = DT("selGc", [16, 128], BF, kind="ExternalInput")
    ident_in = DT("identc", [128, 128], BF, kind="ExternalInput")

    out_d = DT("out", [NTOK, DM], F32, kind="ExternalOutput")
    xnfm_d = DT("xnfm", [NCHUNK, 128, DM], BF, kind="Internal")
    snap_d = DT("snapd", [NCHUNK, 128, DM], BF, kind="Internal")
    cc_in_d = DT("cc_in", [128, 1024], BF, kind="Internal")
    cc_out_d = DT("cc_out", [NCORE * 128, 1024], BF, kind="Internal")

    tap_outs = {}
    es = ExitStack()
    P = Prog(nc, es)
    T = Track()
    for d in ("x", "w", "tp", "tq", "qw", "st", "gp", "xc"):
        P.add_dma_sem(d)

    SB = lambda n, s, dt: nc.alloc_sbuf_tensor(n, s, dt)
    snapR = [SB(f"snapR{i}", [128, 1024], BF) for i in range(2)]
    qfm = SB("qfm", [128, 8 * NTOK], BF)
    AB = SB("AB", [128, NCHUNK * 32], BF)
    Dall = SB("Dall", [128, NCHUNK * 16], BF)
    ones_col = SB("ones_col", [1, 128], BF)
    ones_mov = SB("ones_mov", [128, 1], BF)
    sel32 = SB("sel32", [32, 128], BF)
    msk32 = SB("msk32", [32, 16], F32)
    smbias = SB("smbias", [1, NSM], BF)
    d32 = SB("d32", [32, 4], F32)
    d32m = SB("d32m", [32, 32], BF)
    tiny = SB("tiny", [128, 16], F32)
    bnb = SB("bnb", [128, 12], F32)
    psich = SB("psich", [128, 256], F32)
    pbf = SB("pbf", [128, 16], BF)
    smf2 = [SB(f"smf{i}", [128, NSM], F32) for i in range(2)]
    smb2 = [SB(f"smb{i}", [128, 16], BF) for i in range(2)]
    smg2 = [SB(f"smg{i}", [128, NSM], F32) for i in range(3)]
    cbias = SB("cbias", [128, 2], F32)
    ident = SB("ident", [128, 128], BF)

    base = (nc.sbuf_top - nc.sbuf_bytes_remaining + 31) & ~31
    ov = [base]

    def place(name, shape, dtype):
        t = nc.alloc_sbuf_tensor_at(name, shape, dtype, offset=ov[0])
        sz = int(np.prod(shape[1:])) * mybir.dt.size(dtype)
        ov[0] += (sz + 31) & ~31
        return t

    a0 = ov[0]
    xbuf = [place(f"xbuf{i}", [128, DM], F32) for i in range(2)]
    xn_tm2 = [place(f"xn_tm{i}", [128, DM], BF) for i in range(2)]
    xnsl = [place(f"xnsl{i}", [128, 8 * STOK], BF) for i in range(2)]
    kk2 = [place(f"kk2_{i}", [128, 2048], BF) for i in range(3)]
    vbt = [place(f"vbt{i}", [128, 1024], BF) for i in range(3)]
    psj3 = [place(f"psj{i}", [128, 1024], F32) for i in range(2)]
    wqfull = place("wqfull", [128, 8 * DM], BF)
    wmov = place("wmov", [128, 8 * 2128], BF)
    lam2b = [place(f"lam2b{i}", [128, 32], BF) for i in range(2)]
    assert ov[0] <= nc.sbuf_top, f"A overlay over by {ov[0] - nc.sbuf_top}B"

    ov[0] = a0
    wo_b = place("wo_b", [128, 8 * DM], BF)
    wffn = place("wffn", [128, 8 * DM], BF)
    corr = [place(f"corr{i}", [128, 1024], BF) for i in range(2)]
    snapt = place("snapt", [128, 1024], BF)
    fbias = place("fbias", [1, 1024], BF)
    dloop0 = ov[0]
    xbufD = [place(f"xbufD{i}", [128, DM], F32) for i in range(2)]
    xnfmD = [place(f"xnfmD{i}", [128, DM], BF) for i in range(2)]
    snapD = [place(f"snapD{i}", [128, DM], BF) for i in range(2)]
    o_comb = [place(f"o_comb{i}", [128, 1024], BF) for i in range(2)]
    gbuf = [place(f"gbuf{i}", [128, 1024], BF) for i in range(2)]
    outb = [place(f"outb{i}", [128, DM], F32) for i in range(2)]
    o_fm = place("o_fm", [128, 1024], BF)
    assert ov[0] <= nc.sbuf_top, f"D overlay over by {ov[0] - nc.sbuf_top}B"
    ov[0] = dloop0
    wstageD = place("wstageD", [128, 1024], F32)
    fwrt = place("fwrt", [128, 4 * 512], F32)
    fwit = place("fwit", [128, 4 * 512], F32)
    g1s = place("g1s", [128, 32], F32)
    g2s = place("g2s", [128, 16], F32)
    fttmp = place("fttmp", [128, 512], F32)
    selD = place("selD", [32, 512], BF)
    selG = place("selG", [16, 128], BF)
    mvec = place("mvec", [128, 8], F32)
    g1f = place("g1f", [32, 32], F32)
    g2f = place("g2f", [16, 16], F32)
    g1t = place("g1t", [32, 32], BF)
    g2t = place("g2t", [16, 16], BF)
    gath = [place("gath0", [128, 1024], BF)]
    state_in = place("state_in", [128, 1024], BF)
    assert ov[0] <= nc.sbuf_top, f"D2 overlay over by {ov[0] - nc.sbuf_top}B"

    kps = nc.alloc_psum_tensor("kps", [128, 1024], F32)
    vps = nc.alloc_psum_tensor("vps", [128, 1024], F32)
    smps = nc.alloc_psum_tensor("smps", [128, 512], F32)
    Sps = nc.alloc_psum_tensor("Sps", [128, 1024], F32)
    mps = nc.alloc_psum_tensor("mps", [128, 512], F32)

    tap_n = [0]

    def tap(name, src_name, src_ap):
        if name not in taps:
            return
        names = [src_name] if isinstance(src_name, str) else list(src_name)
        shp = list(src_ap.shape)
        t = DT("tap_" + name, shp, src_ap.dtype, kind="ExternalOutput")
        tap_outs[name] = t
        idx = tuple(slice(0, s) for s in shp)
        tap_n[0] += 1
        waits = []
        for n in names:
            waits += T.rw(n)
        tok = P.dma("sync", "st", t[idx], src_ap, rot=tap_n[0],
                    waits=[w for w in waits if w])
        for n in names:
            T.did_read(n, tok)

    def V(meth, out_name, out_ap, ins, *args, eng="vector", extra_waits=(),
          also_read=(), **kw):
        waits = list(extra_waits) + T.ww(out_name)
        for n, _ in ins:
            waits += T.rw(n)
        for n in also_read:
            waits += T.rw(n)
        m = meth
        if eng == "scalar" and meth == "tensor_copy":
            m = "copy"
        tok = P.op(eng, m, out_ap, *[a for _, a in ins], *args,
                   waits=waits, **kw)
        for n, _ in ins:
            T.did_read(n, tok)
        for n in also_read:
            T.did_read(n, tok)
        T.did_write(out_name, tok)
        return tok

    def DMA(eng, q, dst_name, dst_ap, src_name, src_ap, extra_waits=(),
            rot=0, fresh=True, **kw):
        waits = list(extra_waits) + T.ww(dst_name)
        if src_name:
            waits += T.rw(src_name)
        tok = P.dma(eng, q, dst_ap, src_ap, rot=rot, fresh=fresh,
                    waits=waits, **kw)
        if src_name:
            T.did_read(src_name, tok)
        T.did_write(dst_name, tok)
        return tok

    def mm_group(mms, reads, writes, extra_waits=()):
        waits = list(extra_waits)
        for n in reads:
            waits += T.rw(n)
        for n in writes:
            waits += T.ww(n)
        tok = None
        for i, (o, l, r, st, sp, tp) in enumerate(mms):
            kw = {"start": st, "stop": sp}
            if tp is not None:
                kw["tile_position"] = tp
            tok = P.op("tensor", "matmul", o, l, r, inc=(i == len(mms) - 1),
                       waits=waits if i == 0 else (), **kw)
        for n in reads:
            T.did_read(n, tok)
        for n in writes:
            T.did_write(n, tok)
        return tok

    t = P.op("gpsimd", "memset", ones_col[:, :], 1.0)
    T.did_write("ones_col", t)
    t = P.op("gpsimd", "memset", ones_mov[:, :], 1.0)
    T.did_write("ones_mov", t)
    t = P.op("gpsimd", "memset", cbias[:, 0:1], -2.5)
    T.did_write("cbias", t)
    t = P.op("gpsimd", "memset", cbias[:, 1:2], 0.5)
    T.did_write("cbias", t)
    t = P.op("gpsimd", "memset", psich[:, :], 0.0)
    for _n in ("psi_rk0", "psi_rv0", "psi_dot0", "psi_rk1", "psi_rv1",
               "psi_dot1", "p_den", "p_rden", "p_scr",
               "p_psi", "p_sur", "p_t2", "p_r2", "psi_rkv0", "psi_rkv1",
               "p_f1", "p_f2", "p_f3", "p_f4"):
        T.did_write(_n, t)
    wrot = [0]

    def wnext():
        wrot[0] += 1
        return wrot[0]

    for nm, sb, src in (("sel32", sel32, sel32_in), ("msk32", msk32, msk32_in),
                        ("ident", ident, ident_in)):
        DMA("sync", "w", nm, sb[:, :], None, src[:, :], rot=wnext())

    t = P.op("gpsimd", "memset", smbias[0:1, :], 0.0)
    T.did_write("smbias", t)
    DMA("sync", "w", "tfd", tiny[0:1, 0:16], None, fdb_in[:, :], rot=wnext())
    DMA("sync", "w", "smf0", smf2[0][0:1, 0:16], None, sdb_in[:, :], rot=wnext())
    tok = V("tensor_copy", "smbias", smbias[0:1, 16:32], [("tfd", tiny[0:1, 0:16])])
    tok = V("tensor_copy", "smbias", smbias[0:1, 32:48], [("smf0", smf2[0][0:1, 0:16])])

    for kc in range(8):
        b = kc * 2128
        DMA("sync", "w", "xbuf0", xbuf[0][:, :], None,
            w_in["wk"][kc * 128:(kc + 1) * 128, :], rot=wnext())
        V("tensor_copy", f"wmov_k{kc}", wmov[:, b:b + 1024],
          [("xbuf0", xbuf[0][:, :])])
        DMA("sync", "w", "xbuf0", xbuf[0][:, :], None,
            w_in["wv"][kc * 128:(kc + 1) * 128, :], rot=wnext())
        V("tensor_copy", f"wmov_v{kc}", wmov[:, b + 1024:b + 2048],
          [("xbuf0", xbuf[0][:, :])], eng="scalar")
        srot = wnext()
        for si, snm in enumerate(("wb", "wfd", "wsd", "wfg", "wsg")):
            DMA("sync", "w", "smf0", smf2[0][:, si * 16:(si + 1) * 16], None,
                sm_in[snm][kc * 128:(kc + 1) * 128, :], rot=srot,
                fresh=(si == 0))
        V("tensor_copy", f"wmov_s{kc}", wmov[:, b + 2048:b + 2128],
          [("smf0", smf2[0][:, :])])
    wmov_toks = [T.w[f"wmov_k{k}"] for k in range(8)] + \
        [T.w[f"wmov_v{k}"] for k in range(8)] + \
        [T.w[f"wmov_s{k}"] for k in range(8)]

    for kc in range(8):
        DMA("sync", "w", "xbuf0", xbuf[0][:, :], None,
            w_in["wq"][kc * 128:(kc + 1) * 128, :], rot=wnext())
        V("tensor_copy", f"wq{kc}", wqfull[:, kc * DM:(kc + 1) * DM],
          [("xbuf0", xbuf[0][:, :])], eng="scalar")

    I32 = mybir.dt.int32
    MAGIC = 0x5f3759df

    def rsqrt_chain(a_name, a_ap, r_name, r_ap, t_name, t_ap, iters=2,
                    also=()):
        V("tensor_scalar_max", a_name, a_ap, [(a_name, a_ap)], 1e-30,
          also_read=also)
        V("tensor_single_scalar", r_name, r_ap.bitcast(I32),
          [(a_name, a_ap.bitcast(I32))], 1, ALU.arith_shift_right)
        V("tensor_scalar", r_name, r_ap.bitcast(I32),
          [(r_name, r_ap.bitcast(I32))], -1, MAGIC, ALU.mult, ALU.add)
        for _ in range(iters):
            V("tensor_mul", t_name, t_ap, [(r_name, r_ap), (r_name, r_ap)])
            V("tensor_mul", t_name, t_ap, [(t_name, t_ap), (a_name, a_ap)])
            V("tensor_scalar", t_name, t_ap, [(t_name, t_ap)], -0.5, 1.5,
              ALU.mult, ALU.add)
            V("tensor_mul", r_name, r_ap, [(r_name, r_ap), (t_name, t_ap)])

    smb_bc2 = [smb2[i][:, 0:16].unsqueeze(2).broadcast_to([128, 16, 64])
               for i in range(2)]
    pbf_bc = pbf[:, 0:16].unsqueeze(2).broadcast_to([128, 16, 64])

    C_RK, C_RV, C_DOT = 0, 16, 32
    C_S0, C_SRK = 48, 80
    C_DEN, C_RDEN, C_SCR, C_PSI, C_SUR = 112, 128, 144, 160, 176

    def pc(c0):
        return psich[:, c0:c0 + 16]

    def refined_sqrt(src_name, dst_name, dst_c):
        V("activation", "p_s0", pc(C_S0), [(src_name, pc({"psi_rk": C_RK,
          "psi_rv": C_RV}[src_name]))], AF.Sqrt, eng="scalar")
        V("reciprocal", "p_tmp", pc(C_TMP), [("p_s0", pc(C_S0))])
        V("tensor_mul", "p_tmp", pc(C_TMP),
          [(src_name, pc({"psi_rk": C_RK, "psi_rv": C_RV}[src_name])),
           ("p_tmp", pc(C_TMP))])
        V("tensor_add", "p_tmp", pc(C_TMP),
          [("p_s0", pc(C_S0)), ("p_tmp", pc(C_TMP))])
        V("tensor_scalar", dst_name, pc(dst_c), [("p_tmp", pc(C_TMP))],
          0.5, 1e-8, ALU.mult, ALU.add)

    kk2v = [kk2[i].ap().rearrange("p (h j d) -> p h j d", h=16, j=2)
            for i in range(3)]
    vbtv = [vbt[i].ap().rearrange("p (h d) -> p h d", h=16) for i in range(3)]
    junkv3 = [psj3[i].ap().rearrange("p (h d) -> p h d", h=16)
              for i in range(2)]

    def st_xload(c):
        DMA("sync", "x", f"xbuf{c % 2}", xbuf[c % 2][:, :], None,
            x_in[c * 128:(c + 1) * 128, :], rot=c)

    def st_lnxn(c):
        xname = f"xbuf{c % 2}"
        xb = xbuf[c % 2]
        xn_tm, xn_name = xn_tm2[c % 2], f"xn_tm{c % 2}"
        V("bn_stats", "bnb", bnb[:, 0:6], [(xname, xb[:, 0:512])])
        V("bn_stats", "bnb", bnb[:, 6:12], [(xname, xb[:, 512:1024])])
        V("bn_aggr", "t01", tiny[:, 0:2], [("bnb", bnb[:, 0:12])])
        V("tensor_scalar", "t2", tiny[:, 2:3], [("t01", tiny[:, 1:2])],
          1e-5, None, ALU.add)
        rsqrt_chain("t2", tiny[:, 2:3], "t1r", tiny[:, 1:2],
                    "t4", tiny[:, 4:5])
        V("tensor_scalar", xn_name, xn_tm[:, :], [(xname, xb[:, :])],
          tiny[:, 0:1], tiny[:, 1:2], ALU.subtract, ALU.mult,
          also_read=("t1r", "t01"))
        tap(f"xn{c}", xn_name, xn_tm[:, :])

    def st_xT(c):
        s4, cpos = c // SUP, c % SUP
        sl = xnsl[s4 % 2]
        slname = f"sl{s4 % 2}c{cpos}"
        xn_tm, xn_name = xn_tm2[c % 2], f"xn_tm{c % 2}"
        mms = []
        for kc in range(8):
            mms.append((Sps[:, kc * 128:(kc + 1) * 128],
                        xn_tm[:, kc * 128:(kc + 1) * 128],
                        ident[:, :], True, True, None))
        mm_group(mms, reads=[xn_name, "ident"], writes=["Sps"])
        V("tensor_copy", slname,
          bass.AP(sl, cpos * 128,
                  [[8 * STOK, 128], [STOK, 8], [1, 128]]),
          [("Sps", Sps.ap().rearrange("p (b t) -> p b t", b=8))],
          eng="vector")
        if "noxc" not in skip:
            DMA("sync", "xc", f"xnfmd{c}", xnfm_d[c, :, :], slname,
                bass.AP(sl, cpos * 128, [[8 * STOK, 128], [STOK, 8], [1, 128]]),
                rot=c)

    def st_kvsm(c):
        s4, cpos = c // SUP, c % SUP
        sl = xnsl[s4 % 2]
        slname = f"sl{s4 % 2}c{cpos}"
        mms = []
        for kc in range(8):
            st0, sp7 = (kc == 0), (kc == 7)
            lhs = sl[:, kc * STOK + cpos * 128: kc * STOK + cpos * 128 + 128]
            b = kc * 2128
            mms += [
                (kps[:, 0:512], lhs, wmov[:, b:b + 512], st0, sp7, None),
                (kps[:, 512:1024], lhs, wmov[:, b + 512:b + 1024], st0, sp7, None),
                (vps[:, 0:512], lhs, wmov[:, b + 1024:b + 1536], st0, sp7, None),
                (vps[:, 512:1024], lhs, wmov[:, b + 1536:b + 2048], st0, sp7, None),
                (smps[:, 0:NSM], lhs, wmov[:, b + 2048:b + 2128], st0, False, None),
            ]
        mms.append((smps[:, 0:NSM], ones_col[:, :], smbias[0:1, :],
                    False, True, None))
        mm_group(mms, reads=[slname, "wmov", "ones_col", "smbias"],
                 writes=["kps", "vps", "smps"],
                 extra_waits=wmov_toks if c == 0 else ())

    def st_epi(c):
        p = c % 3
        sp = c % 2
        sg = c % 3
        ka, vn = f"kk2a{p}", f"vbt{p}"
        smfn, smbn, smgn = f"smf{sp}", f"smb{sp}", f"smg{sg}"
        V("activation", smfn, smf2[sp][:, :], [("smps", smps[:, 0:NSM])],
          AF.Tanh, scale=0.5, eng="scalar")
        V("activation", ka, kk2v[p][:, :, 0, :], [("kps", kps[:, :])], AF.Silu,
          eng="scalar")
        V("activation", vn, vbtv[p], [("vps", vps[:, :])], AF.Silu,
          eng="scalar")
        V("activation", smbn, smb2[sp][:, :], [(smfn, smf2[sp][:, 0:16])],
          AF.Copy, bias=0.5, scale=0.5, eng="scalar")
        V("activation", smgn, smg2[sg][:, :], [(smfn, smf2[sp][:, :])],
          AF.Copy, bias=0.5, scale=0.5, eng="scalar")
        tap(f"smf{c}", smgn, smg2[sg][:, :])
        V("tensor_tensor", ka, kk2v[p][:, :, 0, :],
          [(ka, kk2v[p][:, :, 0, :]), (smbn, smb_bc2[sp])], ALU.mult,
          eng="gpsimd")
        V("tensor_tensor", vn, vbtv[p],
          [(vn, vbtv[p]), (smbn, smb_bc2[sp])], ALU.mult, eng="gpsimd")
        tap(f"vb{c}", vn, vbt[p][:, :])

    def st_psi1(c):
        p = c % 3
        rb = 0 if c % 2 == 0 else 192
        ka, vn = f"kk2a{p}", f"vbt{p}"
        kbv = kk2v[p][:, :, 0, :]
        V("activation", "psj0", junkv3[0], [(ka, kbv)], AF.Square,
          eng="scalar")
        V("activation", "psj1", junkv3[1], [(vn, vbtv[p])], AF.Square,
          eng="scalar")
        rk = psich[:, rb + C_RK:rb + C_RK + 16]
        rv = psich[:, rb + C_RV:rb + C_RV + 16]
        dot = psich[:, rb + C_DOT:rb + C_DOT + 16]
        V("reduce_sum", f"psi_rk{c % 2}", rk, [("psj0", junkv3[0])], AXX,
          also_read=(f"psi_rkv{c % 2}",))
        V("reduce_sum", f"psi_rv{c % 2}", rv, [("psj1", junkv3[1])], AXX,
          also_read=(f"psi_rkv{c % 2}",))
        V("tensor_tensor", "psj0", junkv3[0], [(ka, kbv), (vn, vbtv[p])],
          ALU.mult, eng="gpsimd")
        V("reduce_sum", f"psi_dot{c % 2}", dot, [("psj0", junkv3[0])], AXX)

    def st_psi2(c):
        p = c % 3
        sp = c % 2
        sg = c % 3
        rb = 0 if c % 2 == 0 else 192
        rk = psich[:, rb + C_RK:rb + C_RK + 16]
        rv = psich[:, rb + C_RV:rb + C_RV + 16]
        dot = psich[:, rb + C_DOT:rb + C_DOT + 16]
        ka = f"kk2a{p}"
        kbv = kk2v[p][:, :, 0, :]
        V("tensor_mul", "p_den", pc(C_DEN),
          [(f"psi_rk{sp}", rk), (f"psi_rv{sp}", rv)])
        rsqrt_chain("p_den", pc(C_DEN), "p_rden", pc(C_RDEN),
                    "p_t2", psich[:, C_SRK:C_SRK + 16])
        V("activation", "p_scr", pc(C_SCR), [(f"psi_dot{sp}", dot)], AF.Abs,
          eng="scalar")
        V("tensor_mul", "p_scr", pc(C_SCR), [("p_scr", pc(C_SCR)),
                                             ("p_rden", pc(C_RDEN))])
        V("activation", "p_psi", pc(C_PSI), [("p_scr", pc(C_SCR))], AF.Tanh,
          scale=1.5, eng="scalar")
        V("activation", "p_sur", pc(C_SUR), [("p_psi", pc(C_PSI))], AF.Tanh,
          scale=2.5, eng="scalar")
        V("activation", "pbf", pbf[:, :], [("p_sur", pc(C_SUR))], AF.Square,
          bias=cbias[:, 1:2], scale=0.5, eng="scalar", also_read=("cbias",))
        tap(f"psi{c}", (f"psi_rkv{sp}", f"psi_dot{sp}", "p_r2", "p_t2",
                        "p_den", "p_rden", "p_scr", "p_psi", "p_sur",
                        "p_f1", "p_f2", "p_f3", "p_f4"),
            psich[:, rb:rb + 192] if rb == 0 else psich[:, 0:192])
        V("tensor_tensor", f"kk2b{p}", kk2v[p][:, :, 1, :],
          [(ka, kbv), ("pbf", pbf_bc)], ALU.mult, eng="gpsimd")
        tap(f"kb{c}", (ka, f"kk2b{p}"), kk2[p][:, :])
        folds = [("p_f1", C_S0, -0.05, 0.95), ("p_f2", C_S0 + 16, -0.025,
                 0.975), ("p_f3", C_SRK, 0.15, 0.65), ("p_f4", C_SRK + 16,
                 -0.15, 0.35)]
        for fn, col, a, b in folds:
            V("activation", fn, psich[:, col:col + 16],
              [("p_psi", pc(C_PSI))], AF.Copy, bias=b, scale=a, eng="scalar")
        tok = V("tensor_mul", f"psi_rkv{sp}", rk,
                [("p_f1", psich[:, C_S0:C_S0 + 16]),
                 (f"smg{sg}", smg2[sg][:, 16:32])],
                also_read=(f"psi_rk{sp}",), eng="gpsimd")
        T.did_write(f"psi_rk{sp}", tok)
        tok = V("tensor_mul", f"psi_rkv{sp}", rv,
                [("p_f2", psich[:, C_S0 + 16:C_S0 + 32]),
                 (f"smg{sg}", smg2[sg][:, 32:48])],
                also_read=(f"psi_rv{sp}",), eng="gpsimd")
        T.did_write(f"psi_rv{sp}", tok)
        tok = V("tensor_copy", f"lam2b{sp}", lam2b[sp][:, :],
                [(f"psi_rkv{sp}", psich[:, rb + C_RK:rb + C_RK + 32])],
                eng="gpsimd")
        T.did_read(f"psi_rk{sp}", tok)
        T.did_read(f"psi_rv{sp}", tok)
        V("tensor_mul", f"ab{c}", AB[:, 32 * c:32 * c + 16],
          [("p_f3", psich[:, C_SRK:C_SRK + 16]),
           (f"smg{sg}", smg2[sg][:, 48:64])], eng="gpsimd")
        V("tensor_mul", f"ab{c}b", AB[:, 32 * c + 16:32 * c + 32],
          [("p_f4", psich[:, C_SRK + 16:C_SRK + 32]),
           (f"smg{sg}", smg2[sg][:, 64:80])], eng="gpsimd")

    def st_wdsm(c):
        p = c % 2
        mm_group([(mps[0:32, 0:1], lam2b[p][:, :], ones_mov[:, :], True, True,
                   None)],
                 reads=[f"lam2b{p}", "ones_mov"], writes=["mps"])
        b0 = 2 * p
        V("activation", "d32", d32[:, b0:b0 + 1], [("mps", mps[0:32, 0:1])],
          AF.Copy, bias=0.0, scale=1.0 / 128.0, eng="scalar")
        for i in range(7):
            a, bcol = b0 + i % 2, b0 + (i + 1) % 2
            V("activation", "d32", d32[:, bcol:bcol + 1],
              [("d32", d32[:, a:a + 1])], AF.Square, eng="scalar")
        dcol = b0 + 1
        dm = d32m[:, 16 * p:16 * p + 16]
        V("tensor_scalar", f"d32m{p}", dm, [("msk32", msk32[:, :])],
          d32[:, dcol:dcol + 1], None, ALU.mult, also_read=("d32",))
        mm_group([(mps[:, 16:32], sel32[:, :], dm, True, True, None)],
                 reads=["sel32", f"d32m{p}"], writes=["mps2"])
        V("tensor_copy", f"dall{c}", Dall[:, 16 * c:16 * c + 16],
          [("mps2", mps[:, 16:32])], eng="scalar")
        tap(f"dsm{c}", f"dall{c}", Dall[:, 16 * c:16 * c + 16])

    def st_S(c):
        p = c % 3
        mms = []
        for j in range(8):
            h1, h2 = 2 * j, 2 * j + 1
            mms.append((Sps[0:64, j * 128:(j + 1) * 128],
                        vbt[p][:, h1 * 64:(h1 + 1) * 64],
                        kk2[p][:, h1 * 128:(h1 + 1) * 128], True, True, (0, 0)))
            mms.append((Sps[64:128, j * 128:(j + 1) * 128],
                        vbt[p][:, h2 * 64:(h2 + 1) * 64],
                        kk2[p][:, h2 * 128:(h2 + 1) * 128], True, True, (0, 64)))
        mm_group(mms, reads=[f"vbt{p}", f"kk2a{p}", f"kk2b{p}"], writes=["Sps"])

    def st_chain(c):
        sn = f"snapR{c % 2}"
        snap_c = snapR[c % 2][:, :]
        if c == 0:
            V("tensor_copy", sn, snap_c, [("Sps", Sps[:, :])])
        else:
            dview = Dall[:, 16 * c:16 * c + 16] \
                .rearrange("p (j s) -> p j s", j=8) \
                .unsqueeze(3).broadcast_to([128, 8, 2, 64])
            prev = snapR[(c - 1) % 2].ap() \
                .rearrange("p (j s e) -> p j s e", j=8, s=2)
            V("tensor_tensor", sn,
              snap_c.rearrange("p (j s e) -> p j s e", j=8, s=2),
              [(f"snapR{(c - 1) % 2}", prev), (f"dall{c}", dview)], ALU.mult,
              eng="gpsimd")
            V("tensor_add", sn, snap_c, [(sn, snap_c), ("Sps", Sps[:, :])])
        tap(f"snap{c}", sn, snap_c)

    def st_snapst(c):
        DMA("sync", "tp", f"snapd{c}", snap_d[c, :, :], f"snapR{c % 2}",
            snapR[c % 2][:, :], rot=c)

    def st_q(s):
        banks = [kps[:, 0:512], kps[:, 512:1024], vps[:, 0:512],
                 vps[:, 512:1024], smps[:, 0:512], Sps[:, 0:512],
                 Sps[:, 512:1024], mps[:, 0:512]]
        bank_names = ["kps", "kps", "vps", "vps", "smps", "Sps", "Sps", "mps"]
        bank_all = ["kps", "vps", "smps", "Sps", "mps", "mps2"]
        sl = xnsl[s % 2]
        slnames = [f"sl{s % 2}c{i}" for i in range(SUP)]
        for kc in range(8):
            mms = []
            for m in range(8):
                mov = wmov[:, 0:STOK] if "qdummy" in skip \
                    else sl[:, kc * STOK:(kc + 1) * STOK]
                mms.append((banks[m],
                            wqfull[:, kc * DM + m * 128:kc * DM + (m + 1) * 128],
                            mov, kc == 0, kc == 7, None))
            rds = (["wmov"] if "qdummy" in skip else slnames) + \
                [f"wq{kc}"]
            grp = mm_group(mms, reads=rds,
                           writes=(bank_all if kc == 0 else []),
                           extra_waits=([T.w.get("qgrp")] if kc > 0 else ()))
            T.did_write("qgrp", grp)
            if kc > 0:
                for n in bank_all:
                    T.did_write(n, grp)
        if "qnosil" in skip:
            return
        for m in range(8):
            V("activation", f"qfm{m}",
              qfm[:, m * NTOK + s * STOK: m * NTOK + (s + 1) * STOK],
              [(bank_names[m], banks[m])], AF.Silu, eng="scalar")
        tap(f"q{s}", "qfm0",
            qfm[:, 0 * NTOK + s * STOK: 0 * NTOK + (s + 1) * STOK])

    st_xload(0)
    st_lnxn(0)
    st_xT(0)
    st_xload(1)
    st_lnxn(1)
    st_xload(2)
    for c in range(NCHUNK + 2):
        if c + 3 < NCHUNK:
            st_xload(c + 3)
        if c + 2 < NCHUNK:
            st_lnxn(c + 2)
        if c >= 3:
            st_snapst(c - 3)
        if 0 <= c - 2 < NCHUNK:
            st_psi2(c - 2)
        if c < NCHUNK:
            st_kvsm(c)
            st_epi(c)
        if c + 1 < NCHUNK:
            st_xT(c + 1)
        if c < NCHUNK:
            st_psi1(c)
        if c >= 2:
            st_wdsm(c - 2)
            st_S(c - 2)
            st_chain(c - 2)
        if c >= 1 and (c - 1) % SUP == SUP - 1 and "q" not in skip:
            st_q((c - 1) // SUP)
    st_snapst(NCHUNK - 1)

    if use_cc:
        DMA("gpsimd", "gp", "cc_in", cc_in_d[:, :], f"snapR{(NCHUNK - 1) % 2}",
            snapR[(NCHUNK - 1) % 2][:, :], rot=0)
        t_cc = P.op("gpsimd", "collective_compute", "AllGather", ALU.bypass,
                    [[i for i in range(NCORE)]],
                    [cc_in_d.ap().opt()], [cc_out_d.ap().opt()],
                    waits=T.rw("cc_in") + [T.w.get("cc_in")])
        T.did_write("cc_out", t_cc)

    P.fence_all()

    for nm, sb, src in (("selD", selD, selD_in), ("selG", selG, selG_in),
                        ("mvec", mvec, mvec_in)):
        DMA("sync", "w", nm, sb[:, :], None, src[:, :], rot=wnext())
    DMA("sync", "w", "wsD", wstageD[0:1, 0:512], None, fbr_in[:, :],
        rot=wnext())
    V("tensor_copy", "fbias", fbias[0:1, 0:512], [("wsD", wstageD[0:1, 0:512])])
    DMA("sync", "w", "wsD", wstageD[0:1, 0:512], None, fbi_in[:, :],
        rot=wnext())
    V("tensor_copy", "fbias2", fbias[0:1, 512:1024],
      [("wsD", wstageD[0:1, 0:512])])

    for kc in range(8):
        DMA("sync", "w", "wsD", wstageD[:, :], None,
            w_in["wo"][kc * 128:(kc + 1) * 128, :], rot=wnext())
        V("tensor_copy", f"wo{kc}", wo_b[:, kc * 1024:(kc + 1) * 1024],
          [("wsD", wstageD[:, :])], eng="scalar")
    wo_toks = [T.w[f"wo{k}"] for k in range(8)]

    for which, g1src, g2src, dest, dname in (
            ("r", g1r_in, g2r_in, fwrt, "fwrt"), ("i", g1i_in, g2i_in, fwit,
                                                  "fwit")):
        for c4 in range(4):
            DMA("sync", "w", "g1f", g1f[:, :], None,
                bass.AP(g1src, c4 * 32, [[1, 32], [128, 32]]),
                allow_slow_non_contiguous=True, rot=wnext())
            DMA("sync", "w", "g2f", g2f[:, :], None,
                bass.AP(g2src, c4 * 256, [[1, 16], [16, 16]]),
                allow_slow_non_contiguous=True, rot=wnext())
            V("tensor_copy", "g1t", g1t[:, :], [("g1f", g1f[:, :])])
            V("tensor_copy", "g2t", g2t[:, :], [("g2f", g2f[:, :])])
            mm_group([(mps[:, 32:48], selG[:, :], g2t[:, :], True, True, None)],
                     reads=["selG", "g2t"], writes=["mpsB"])
            V("tensor_copy", "g2s", g2s[:, :], [("mpsB", mps[:, 32:48])])
            for t4 in range(4):
                mm_group([(mps[:, 0:32], selD[:, t4 * 128:(t4 + 1) * 128],
                           g1t[:, :], True, True, None)],
                         reads=["selD", "g1t"], writes=["mpsA"])
                V("tensor_copy", "g1s", g1s[:, :], [("mpsA", mps[:, 0:32])])
                a_bc = g1s.ap().unsqueeze(2).broadcast_to([128, 32, 16])
                b_bc = g2s.ap().unsqueeze(1).broadcast_to([128, 32, 16])
                dst = dest[:, t4 * 512:(t4 + 1) * 512] \
                    .rearrange("p (b e) -> p b e", b=32)
                if c4 == 0:
                    V("tensor_tensor", f"{dname}{t4}", dst,
                      [("g1s", a_bc), ("g2s", b_bc)], ALU.mult)
                else:
                    V("tensor_tensor", "fttmp",
                      fttmp.ap().rearrange("p (b e) -> p b e", b=32),
                      [("g1s", a_bc), ("g2s", b_bc)], ALU.mult)
                    V("tensor_add", f"{dname}{t4}", dst,
                      [(f"{dname}{t4}", dst), ("fttmp", fttmp.ap()
                        .rearrange("p (b e) -> p b e", b=32))])
    for kc in range(8):
        b = kc * 1024
        if kc < 4:
            V("tensor_copy", f"wffn{kc}a", wffn[:, b:b + 512],
              [(f"fwrt{kc}", fwrt[:, kc * 512:(kc + 1) * 512])])
            V("tensor_copy", f"wffn{kc}b", wffn[:, b + 512:b + 1024],
              [(f"fwit{kc}", fwit[:, kc * 512:(kc + 1) * 512])], eng="scalar")
        else:
            V("tensor_scalar", f"wffn{kc}a", wffn[:, b:b + 512],
              [(f"fwit{kc - 4}", fwit[:, (kc - 4) * 512:(kc - 3) * 512])],
              -1.0, None, ALU.mult)
            V("tensor_copy", f"wffn{kc}b", wffn[:, b + 512:b + 1024],
              [(f"fwrt{kc - 4}", fwrt[:, (kc - 4) * 512:(kc - 3) * 512])],
              eng="scalar")
    wffn_toks = [T.w[f"wffn{k}a"] for k in range(8)] + \
        [T.w[f"wffn{k}b"] for k in range(8)]

    if not use_cc:
        t = P.op("gpsimd", "memset", state_in[:, :], 0.0)
        T.did_write("state_in", t)
    for j in range(NCORE if use_cc else 0):
        DMA("gpsimd", "gp", "gath0", gath[0][:, :], "cc_out",
            cc_out_d[j * 128:(j + 1) * 128, :], rot=j)
        pp = [corr[0], state_in][j % 2]
        ppn = ["corr0", "state_in"][j % 2]
        prevn = ["corr0", "state_in"][(j + 1) % 2]
        if j == 0:
            V("tensor_scalar", ppn, pp[:, :], [("gath0", gath[0][:, :])],
              mvec[:, 0:1], None, ALU.mult, also_read=("mvec",))
        else:
            V("scalar_tensor_tensor", ppn, pp[:, :],
              [("gath0", gath[0][:, :])],
              mvec[:, j:j + 1],
              [corr[0], state_in][(j + 1) % 2][:, :],
              ALU.mult, ALU.add, also_read=(prevn, "mvec"))
    tap("state_in", "state_in", state_in[:, :])
    P.fence_all()

    kps8 = kps.ap().rearrange("p (h j d) -> p h j d", h=8, j=2)
    vps8 = vps.ap().rearrange("p (h j d) -> p h j d", h=8, j=2)

    def st_xD(c):
        DMA("sync", "x", f"xbufD{c % 2}", xbufD[c % 2][:, :], None,
            x_in[c * 128:(c + 1) * 128, :], rot=c)

    def st_xnfm(c):
        DMA("sync", "xc", f"xnfmD{c % 2}", xnfmD[c % 2][:, :], f"xnfmd{c}",
            xnfm_d[c, :, :], rot=c)

    def st_snapld(c):
        DMA("sync", "tq", f"snapD{c % 2}", snapD[c % 2][:, :], f"snapd{c}",
            snap_d[c, :, :], rot=c)

    def st_corr(c):
        dview = Dall[:, 16 * c:16 * c + 16] \
            .rearrange("p (j s) -> p j s", j=8) \
            .unsqueeze(3).broadcast_to([128, 8, 2, 64])
        cb, cbn = corr[c % 2], f"corr{c % 2}"
        pv, pvn = (state_in, "state_in") if c == 0 else \
            (corr[(c + 1) % 2], f"corr{(c + 1) % 2}")
        tok = V("tensor_tensor", cbn,
                cb.ap().rearrange("p (j s e) -> p j s e", j=8, s=2),
                [(pvn, pv.ap().rearrange("p (j s e) -> p j s e", j=8, s=2)),
                 (f"dall{c}", dview)], ALU.mult)
        if c == 0:
            T.did_read("outb0", tok)
        V("tensor_add", "snapt", snapt[:, :],
          [(f"snapD{c % 2}", snapD[c % 2][:, :]), (cbn, cb[:, :])])

    def st_ffn(c):
        p = c % 2
        mms = []
        for kc in range(8):
            st0 = kc == 0
            lhs = xnfmD[p][:, kc * 128:(kc + 1) * 128]
            b = kc * 1024
            mms.append((smps[:, 0:512], lhs, wffn[:, b:b + 512], st0, False,
                        None))
            mms.append((mps[:, 0:512], lhs, wffn[:, b + 512:b + 1024], st0,
                        False, None))
        mms.append((smps[:, 0:512], ones_col[:, :], fbias[0:1, 0:512], False,
                    True, None))
        mms.append((mps[:, 0:512], ones_col[:, :], fbias[0:1, 512:1024], False,
                    True, None))
        mm_group(mms, reads=[f"xnfmD{p}", "ones_col", "fbias", "fbias2"],
                 writes=["smps", "mps", "mpsA", "mpsB"],
                 extra_waits=(wffn_toks if c == 0 else ()))
        V("activation", f"gbuf{p}", gbuf[p][:, 0:512],
          [("smps", smps[:, 0:512])], AF.Silu, eng="scalar")
        V("activation", f"gbuf{p}", gbuf[p][:, 512:1024],
          [("mps", mps[:, 0:512])], AF.Silu, eng="scalar")
        tap(f"g{c}", f"gbuf{p}", gbuf[p][:, :])

    def st_o(c):
        mms = []
        for j in range(8):
            for hh in range(2):
                rows = slice(hh * 64, hh * 64 + 64)
                outap = [kps, vps][hh][:, j * 128:(j + 1) * 128]
                mms.append((outap,
                            qfm[rows, j * NTOK + c * 128: j * NTOK + (c + 1) * 128],
                            snapt[rows, j * 128:(j + 1) * 128],
                            True, True, None))
        mm_group(mms, reads=["snapt"] + [f"qfm{j}" for j in range(8)],
                 writes=["kps", "vps"])

    def st_gate(c):
        p = c % 2
        on = f"o_comb{p}"
        for par, psv, psn in ((0, kps8, "kps"), (1, vps8, "vps")):
            a_bc = bass.AP(AB, 32 * c + par,
                           [[AB.shape[1], 128], [2, 8], [0, 64]])
            b_bc = bass.AP(AB, 32 * c + 16 + par,
                           [[AB.shape[1], 128], [2, 8], [0, 64]])
            oc = bass.AP(o_comb[p], par * 64,
                         [[o_comb[p].shape[1], 128], [128, 8], [1, 64]])
            tmp = bass.AP(o_fm, par * 64,
                          [[o_fm.shape[1], 128], [128, 8], [1, 64]])
            V("tensor_tensor", on, oc,
              [(psn, psv[:, :, 0, :]), (f"ab{c}", a_bc)], ALU.mult)
            V("tensor_tensor", "o_fm", tmp,
              [(psn, psv[:, :, 1, :]), (f"ab{c}b", b_bc)], ALU.mult)
            V("tensor_add", on, oc, [(on, oc), ("o_fm", tmp)])
        tap(f"oc{c}", on, o_comb[p][:, :])

    def st_oT(c):
        p = c % 2
        mms = []
        for kc in range(8):
            mms.append((kps[:, kc * 128:(kc + 1) * 128],
                        o_comb[p][:, kc * 128:(kc + 1) * 128],
                        ident[:, :], True, True, None))
        mm_group(mms, reads=[f"o_comb{p}", "ident"], writes=["kps"])
        V("tensor_copy", "o_fm", o_fm[:, 0:512], [("kps", kps[:, 0:512])],
          eng="scalar")
        V("tensor_copy", "o_fm", o_fm[:, 512:1024], [("kps", kps[:, 512:1024])],
          eng="scalar")

    def st_wo(c):
        mms = []
        for kc in range(8):
            st0, sp7 = kc == 0, kc == 7
            lhs = o_fm[:, kc * 128:(kc + 1) * 128]
            b = kc * 1024
            mms.append((Sps[:, 0:512], lhs, wo_b[:, b:b + 512], st0, sp7, None))
            mms.append((Sps[:, 512:1024], lhs, wo_b[:, b + 512:b + 1024], st0,
                        sp7, None))
        mm_group(mms, reads=["o_fm"] + ([f"wo{k}" for k in range(8)]
                                        if c == 0 else []),
                 writes=["Sps"])

    def st_final(c):
        p = c % 2
        ob, obn = outb[p], f"outb{p}"
        V("tensor_tensor", obn, ob[:, :],
          [(f"gbuf{p}", gbuf[p][:, :]), ("Sps", Sps[:, :])], ALU.mult)
        V("tensor_add", obn, ob[:, :], [(obn, ob[:, :]),
                                        (f"xbufD{p}", xbufD[p][:, :])])
        DMA("sync", "st", f"out{c}", out_d[c * 128:(c + 1) * 128, :], obn,
            ob[:, :], rot=c)

    st_snapld(0)
    st_xnfm(0)
    st_corr(0)
    for c in range(0 if "d" not in skip else NCHUNK + 2, NCHUNK + 2):
        if c + 1 < NCHUNK:
            st_snapld(c + 1)
        if 0 <= c - 1 < NCHUNK:
            st_xD(c - 1)
        if c + 1 < NCHUNK:
            st_xnfm(c + 1)
        if c - 2 >= 0:
            st_final(c - 2)
        if c < NCHUNK:
            st_ffn(c)
        if c - 1 >= 0 and c - 1 < NCHUNK:
            st_oT(c - 1)
            st_wo(c - 1)
        if c < NCHUNK:
            st_o(c)
            st_gate(c)
            if c + 1 < NCHUNK:
                st_corr(c + 1)

    if "d" in skip:
        t = P.op("gpsimd", "memset", outb[0][:, :], 0.0)
        T.did_write("outb0", t)
        for c in range(NCHUNK):
            DMA("sync", "st", f"out{c}", out_d[c * 128:(c + 1) * 128, :],
                "outb0", out_d is not None and outb[0][:, :], rot=c, fresh=(c < 2))
    P.final_wait("sync", "st")
    P.final_wait("gpsimd", "gp")

    with nc.Block() as block, es:
        P.build(block)
    return nc, tap_outs



def make_consts(core):
    sel32 = np.zeros((32, 128), dtype=bf16)
    for k in range(32):
        h = k % 16
        par = h % 2
        sel32[k, par * 64:(par + 1) * 64] = 1.0
    sel32[:] = 0
    for k in range(32):
        h = k % 16
        for p in range(128):
            if h % 2 == p // 64:
                sel32[k, p] = 1.0
    msk32 = np.zeros((32, 16), dtype=np.float32)
    for k in range(32):
        s, h = k // 16, k % 16
        j = h // 2
        msk32[k, j * 2 + s] = 1.0
    mvec = np.zeros((128, 8), dtype=np.float32)
    if core % 2 == 1:
        mvec[:, core - 1] = 1.0
    selD = np.zeros((32, 512), dtype=bf16)
    for t4 in range(4):
        for p in range(128):
            selD[8 * t4 + p // 16, t4 * 128 + p] = 1.0
    selG = np.zeros((16, 128), dtype=bf16)
    for p in range(128):
        selG[p % 16, p] = 1.0
    return sel32, msk32, mvec, selD, selG


def core_inputs(core, inp):
    b, half = core // 2, core % 2
    x = np.ascontiguousarray(
        np.asarray(inp["L0_attn_hidden_states"])[b, half * NTOK:(half + 1) * NTOK, :])
    sel32, msk32, mvec, selD, selG = make_consts(core)
    asF = lambda a: np.ascontiguousarray(np.asarray(a, dtype=np.float32))
    return {
        "x": x.astype(np.float32),
        "wq": asF(inp["L0_attn_Wq"]), "wk": asF(inp["L0_attn_Wk"]),
        "wv": asF(inp["L0_attn_Wv"]), "wo": asF(inp["L0_attn_Wo"]),
        "wb": asF(inp["L0_attn_Wb"]), "wfd": asF(inp["L0_attn_Wfd"]),
        "wsd": asF(inp["L0_attn_Wsd"]), "wfg": asF(inp["L0_attn_Wfg"]),
        "wsg": asF(inp["L0_attn_Wsg"]),
        "fdb": asF(inp["L0_attn_fd_bias"]).reshape(1, H),
        "sdb": asF(inp["L0_attn_sd_bias"]).reshape(1, H),
        "g1r": asF(inp["L0_ffn_G1_R"]).reshape(32, 4, 32),
        "g2r": asF(inp["L0_ffn_G2_R"]).reshape(4, 16, 16),
        "g1i": asF(inp["L0_ffn_G1_I"]).reshape(32, 4, 32),
        "g2i": asF(inp["L0_ffn_G2_I"]).reshape(4, 16, 16),
        "fbr": asF(inp["L0_ffn_bias_real"]).reshape(1, 512),
        "fbi": asF(inp["L0_ffn_bias_imag"]).reshape(1, 512),
        "sel32c": sel32, "msk32c": msk32, "mvecc": mvec,
        "selDc": selD, "selGc": selG,
        "identc": np.eye(128, dtype=bf16),
    }


def kernel(**inputs):
    nc, _ = build_nc()
    in_maps = [core_inputs(i, inputs) for i in range(NCORE)]
    res = run_bass_kernel_spmd(nc, in_maps, core_ids=list(range(NCORE)))
    outs = [res.results[i]["out"] for i in range(NCORE)]
    full = np.empty((4, 8192, DM), dtype=np.float32)
    for i in range(NCORE):
        b, half = i // 2, i % 2
        full[b, half * NTOK:(half + 1) * NTOK, :] = outs[i]
    return full


# version 26
# speedup vs baseline: 1.0531x; 1.0280x over previous
import numpy as np
import ml_dtypes
import concourse.bass as bass
import concourse.mybir as mybir
from concourse.bass_utils import run_bass_kernel_spmd
from contextlib import ExitStack

BF = mybir.dt.bfloat16
F32 = mybir.dt.float32
AF = mybir.ActivationFunctionType
ALU = mybir.AluOpType
AXX = mybir.AxisListType.X

NCORE = 8
DM = 1024
H, HD = 16, 64
CHUNK = 128
NCHUNK = 32
NTOK = NCHUNK * CHUNK
SUP = 4
NSUP = NCHUNK // SUP
STOK = SUP * CHUNK
NSM = 80

bf16 = ml_dtypes.bfloat16
ENGS = ("sync", "scalar", "vector", "tensor", "gpsimd")


def set_dims(nchunk):
    global NCHUNK, NTOK, NSUP
    NCHUNK = nchunk
    NTOK = NCHUNK * CHUNK
    NSUP = NCHUNK // SUP
    assert NCHUNK % SUP == 0


class Prog:
    def __init__(self, nc, es):
        self.nc = nc
        self.sem = {e: es.enter_context(nc.semaphore(f"T{e}")) for e in ENGS}
        self.cnt = {e: 0 for e in ENGS}
        self.stream = {e: [] for e in ENGS}
        self.water = {e: {} for e in ENGS}
        self.dsem, self.dcnt = {}, {}
        self.es = es

    def add_dma_sem(self, name):
        for r in range(2):
            nm = f"{name}{r}"
            self.dsem[nm] = self.es.enter_context(self.nc.semaphore(f"D{nm}"))
            self.dcnt[nm] = 0

    def _wait(self, e, tok):
        key, sem, val = tok
        if self.water[e].get(key, 0) >= val:
            return
        self.water[e][key] = val
        self.stream[e].append(("w", sem, val))

    def _do_waits(self, e, waits):
        best = {}
        for t in waits:
            if t is None:
                continue
            k = t[0]
            if k not in best or t[2] > best[k][2]:
                best[k] = t
        for t in best.values():
            self._wait(e, t)

    def op(self, e, meth, *args, waits=(), inc=True, **kw):
        self._do_waits(e, waits)
        if inc:
            self.cnt[e] += 1
            self.stream[e].append(("i", meth, args, kw, self.sem[e], 1))
        else:
            self.stream[e].append(("o", meth, args, kw))
        return ("E" + e, self.sem[e], self.cnt[e])

    def dma(self, e, dname, out, in_, rot=0, fresh=True, waits=(), **kw):
        nm = f"{dname}{rot % 2}"
        s = self.dsem[nm]
        if fresh and self.dcnt[nm] > 0:
            self._wait(e, ("D" + nm, s, self.dcnt[nm]))
        self._do_waits(e, waits)
        self.dcnt[nm] += 16
        self.stream[e].append(("i", "dma_start", (), dict(out=out, in_=in_, **kw),
                               s, 16))
        return ("D" + nm, s, self.dcnt[nm])

    def tok(self, e):
        return ("E" + e, self.sem[e], self.cnt[e])

    def fence_all(self):
        toks = [self.tok(e) for e in ENGS] + \
            [("D" + d, self.dsem[d], self.dcnt[d]) for d in self.dsem]
        for e in ENGS:
            for t in toks:
                if t[2] > 0 and t[0] != "E" + e:
                    self._wait(e, t)

    def final_wait(self, e, dname):
        for r in range(2):
            nm = f"{dname}{r}"
            if self.dcnt[nm] > 0:
                self.stream[e].append(("w", self.dsem[nm], self.dcnt[nm]))

    def build(self, block):
        for e in ENGS:
            ops = self.stream[e]

            def runner(eng, ops=ops):
                for it in ops:
                    if it[0] == "w":
                        eng.wait_ge(it[1], it[2])
                    elif it[0] == "o":
                        getattr(eng, it[1])(*it[2], **it[3])
                    else:
                        getattr(eng, it[1])(*it[2], **it[3]).then_inc(it[4], it[5])
            getattr(block, e)(runner)


class Track:
    def __init__(self):
        self.w = {}
        self.r = {}

    def rw(self, name):
        return [self.w.get(name)]

    def ww(self, name):
        return self.r.get(name, []) + [self.w.get(name)]

    def did_read(self, name, tok):
        self.r.setdefault(name, []).append(tok)

    def did_write(self, name, tok):
        self.w[name] = tok
        self.r[name] = []


def build_nc(taps=(), use_cc=True, skip=()):
    skip = set(skip)
    taps = set(taps)
    nc = bass.Bass("TRN2", target_bir_lowering=False, debug=False)
    DT = nc.dram_tensor

    x_in = DT("x", [NTOK, DM], F32, kind="ExternalInput")
    w_in = {n: DT(n, [DM, DM], F32, kind="ExternalInput")
            for n in ("wq", "wk", "wv", "wo")}
    sm_in = {n: DT(n, [DM, H], F32, kind="ExternalInput")
             for n in ("wb", "wfd", "wsd", "wfg", "wsg")}
    fdb_in = DT("fdb", [1, H], F32, kind="ExternalInput")
    sdb_in = DT("sdb", [1, H], F32, kind="ExternalInput")
    g1r_in = DT("g1r", [32, 4, 32], F32, kind="ExternalInput")
    g2r_in = DT("g2r", [4, 16, 16], F32, kind="ExternalInput")
    g1i_in = DT("g1i", [32, 4, 32], F32, kind="ExternalInput")
    g2i_in = DT("g2i", [4, 16, 16], F32, kind="ExternalInput")
    fbr_in = DT("fbr", [1, 512], F32, kind="ExternalInput")
    fbi_in = DT("fbi", [1, 512], F32, kind="ExternalInput")
    sel32_in = DT("sel32c", [32, 128], BF, kind="ExternalInput")
    msk32_in = DT("msk32c", [32, 16], F32, kind="ExternalInput")
    mvec_in = DT("mvecc", [128, 8], F32, kind="ExternalInput")
    selD_in = DT("selDc", [32, 512], BF, kind="ExternalInput")
    selG_in = DT("selGc", [16, 128], BF, kind="ExternalInput")
    ident_in = DT("identc", [128, 128], BF, kind="ExternalInput")

    out_d = DT("out", [NTOK, DM], F32, kind="ExternalOutput")
    xnfm_d = DT("xnfm", [NCHUNK, 128, DM], BF, kind="Internal")
    snap_d = DT("snapd", [NCHUNK, 128, DM], BF, kind="Internal")
    cc_in_d = DT("cc_in", [128, 1024], BF, kind="Internal")
    cc_out_d = DT("cc_out", [NCORE * 128, 1024], BF, kind="Internal")

    tap_outs = {}
    es = ExitStack()
    P = Prog(nc, es)
    T = Track()
    for d in ("x", "w", "tp", "tq", "qw", "st", "gp", "xc", "sm"):
        P.add_dma_sem(d)

    SB = lambda n, s, dt: nc.alloc_sbuf_tensor(n, s, dt)
    snapR = [SB(f"snapR{i}", [128, 1024], BF) for i in range(2)]
    qfm = SB("qfm", [128, 8 * NTOK], BF)
    AB = SB("AB", [128, NCHUNK * 32], BF)
    Dall = SB("Dall", [128, NCHUNK * 16], BF)
    ones_col = SB("ones_col", [1, 128], BF)
    ones_mov = SB("ones_mov", [128, 1], BF)
    sel32 = SB("sel32", [32, 128], BF)
    msk32 = SB("msk32", [32, 16], F32)
    smbias = SB("smbias", [1, NSM], BF)
    d32 = SB("d32", [32, 4], F32)
    d32m = SB("d32m", [32, 32], BF)
    tiny = SB("tiny", [128, 16], F32)
    bnb = SB("bnb", [128, 12], F32)
    psich = SB("psich", [128, 256], F32)
    pbf = SB("pbf", [128, 16], BF)
    smf2 = [SB(f"smf{i}", [128, NSM], F32) for i in range(2)]
    smb2 = [SB(f"smb{i}", [128, 16], BF) for i in range(2)]
    smg2 = [SB(f"smg{i}", [128, NSM], F32) for i in range(3)]
    cbias = SB("cbias", [128, 2], F32)
    ident = SB("ident", [128, 128], BF)

    base = (nc.sbuf_top - nc.sbuf_bytes_remaining + 31) & ~31
    ov = [base]

    def place(name, shape, dtype):
        t = nc.alloc_sbuf_tensor_at(name, shape, dtype, offset=ov[0])
        sz = int(np.prod(shape[1:])) * mybir.dt.size(dtype)
        ov[0] += (sz + 31) & ~31
        return t

    a0 = ov[0]
    xbuf = [place(f"xbuf{i}", [128, DM], F32) for i in range(2)]
    xn_tm2 = [place(f"xn_tm{i}", [128, DM], BF) for i in range(2)]
    xnsl = [place(f"xnsl{i}", [128, 8 * STOK], BF) for i in range(2)]
    kk2 = [place(f"kk2_{i}", [128, 2048], BF) for i in range(3)]
    vbt = [place(f"vbt{i}", [128, 1024], BF) for i in range(3)]
    psj3 = [place(f"psj{i}", [128, 1024], F32) for i in range(2)]
    wqfull = place("wqfull", [128, 8 * DM], BF)
    wmov = place("wmov", [128, 8 * 2128], BF)
    lam2b = [place(f"lam2b{i}", [128, 32], BF) for i in range(2)]
    assert ov[0] <= nc.sbuf_top, f"A overlay over by {ov[0] - nc.sbuf_top}B"

    ov[0] = a0
    wo_b = place("wo_b", [128, 8 * DM], BF)
    wffn = place("wffn", [128, 8 * DM], BF)
    corr = [place(f"corr{i}", [128, 1024], BF) for i in range(2)]
    snapt = place("snapt", [128, 1024], BF)
    fbias = place("fbias", [1, 1024], BF)
    dloop0 = ov[0]
    xbufD = [place(f"xbufD{i}", [128, DM], F32) for i in range(2)]
    xnfmD = [place(f"xnfmD{i}", [128, DM], BF) for i in range(2)]
    snapD = [place(f"snapD{i}", [128, DM], BF) for i in range(2)]
    o_comb = [place(f"o_comb{i}", [128, 1024], BF) for i in range(2)]
    gbuf = [place(f"gbuf{i}", [128, 1024], BF) for i in range(2)]
    outb = [place(f"outb{i}", [128, DM], F32) for i in range(2)]
    o_fm = place("o_fm", [128, 1024], BF)
    assert ov[0] <= nc.sbuf_top, f"D overlay over by {ov[0] - nc.sbuf_top}B"
    ov[0] = dloop0
    wstageD = place("wstageD", [128, 1024], F32)
    fwrt = place("fwrt", [128, 4 * 512], F32)
    fwit = place("fwit", [128, 4 * 512], F32)
    g1s = place("g1s", [128, 32], F32)
    g2s = place("g2s", [128, 16], F32)
    fttmp = place("fttmp", [128, 512], F32)
    selD = place("selD", [32, 512], BF)
    selG = place("selG", [16, 128], BF)
    mvec = place("mvec", [128, 8], F32)
    g1f = place("g1f", [32, 32], F32)
    g2f = place("g2f", [16, 16], F32)
    g1t = place("g1t", [32, 32], BF)
    g2t = place("g2t", [16, 16], BF)
    gath = [place("gath0", [128, 1024], BF)]
    state_in = place("state_in", [128, 1024], BF)
    assert ov[0] <= nc.sbuf_top, f"D2 overlay over by {ov[0] - nc.sbuf_top}B"

    kps = nc.alloc_psum_tensor("kps", [128, 1024], F32)
    vps = nc.alloc_psum_tensor("vps", [128, 1024], F32)
    smps = nc.alloc_psum_tensor("smps", [128, 512], F32)
    Sps = nc.alloc_psum_tensor("Sps", [128, 1024], F32)
    mps = nc.alloc_psum_tensor("mps", [128, 512], F32)

    tap_n = [0]

    def tap(name, src_name, src_ap):
        if name not in taps:
            return
        names = [src_name] if isinstance(src_name, str) else list(src_name)
        shp = list(src_ap.shape)
        t = DT("tap_" + name, shp, src_ap.dtype, kind="ExternalOutput")
        tap_outs[name] = t
        idx = tuple(slice(0, s) for s in shp)
        tap_n[0] += 1
        waits = []
        for n in names:
            waits += T.rw(n)
        tok = P.dma("sync", "st", t[idx], src_ap, rot=tap_n[0],
                    waits=[w for w in waits if w])
        for n in names:
            T.did_read(n, tok)

    def V(meth, out_name, out_ap, ins, *args, eng="vector", extra_waits=(),
          also_read=(), **kw):
        waits = list(extra_waits) + T.ww(out_name)
        for n, _ in ins:
            waits += T.rw(n)
        for n in also_read:
            waits += T.rw(n)
        m = meth
        if eng == "scalar" and meth == "tensor_copy":
            m = "copy"
        tok = P.op(eng, m, out_ap, *[a for _, a in ins], *args,
                   waits=waits, **kw)
        for n, _ in ins:
            T.did_read(n, tok)
        for n in also_read:
            T.did_read(n, tok)
        T.did_write(out_name, tok)
        return tok

    def DMA(eng, q, dst_name, dst_ap, src_name, src_ap, extra_waits=(),
            rot=0, fresh=True, **kw):
        waits = list(extra_waits) + T.ww(dst_name)
        if src_name:
            waits += T.rw(src_name)
        tok = P.dma(eng, q, dst_ap, src_ap, rot=rot, fresh=fresh,
                    waits=waits, **kw)
        if src_name:
            T.did_read(src_name, tok)
        T.did_write(dst_name, tok)
        return tok

    def mm_group(mms, reads, writes, extra_waits=()):
        waits = list(extra_waits)
        for n in reads:
            waits += T.rw(n)
        for n in writes:
            waits += T.ww(n)
        tok = None
        for i, (o, l, r, st, sp, tp) in enumerate(mms):
            kw = {"start": st, "stop": sp}
            if tp is not None:
                kw["tile_position"] = tp
            tok = P.op("tensor", "matmul", o, l, r, inc=(i == len(mms) - 1),
                       waits=waits if i == 0 else (), **kw)
        for n in reads:
            T.did_read(n, tok)
        for n in writes:
            T.did_write(n, tok)
        return tok

    t = P.op("gpsimd", "memset", ones_col[:, :], 1.0)
    T.did_write("ones_col", t)
    t = P.op("gpsimd", "memset", ones_mov[:, :], 1.0)
    T.did_write("ones_mov", t)
    t = P.op("gpsimd", "memset", cbias[:, 0:1], -2.5)
    T.did_write("cbias", t)
    t = P.op("gpsimd", "memset", cbias[:, 1:2], 0.5)
    T.did_write("cbias", t)
    t = P.op("gpsimd", "memset", psich[:, :], 0.0)
    for _n in ("psi_rk0", "psi_rv0", "psi_dot0", "psi_rk1", "psi_rv1",
               "psi_dot1", "p_den", "p_rden", "p_scr",
               "p_psi", "p_sur", "p_t2", "p_r2", "psi_rkv0", "psi_rkv1",
               "p_f1", "p_f2", "p_f3", "p_f4"):
        T.did_write(_n, t)
    wrot = [0]

    def wnext():
        wrot[0] += 1
        return wrot[0]

    for nm, sb, src in (("sel32", sel32, sel32_in), ("msk32", msk32, msk32_in),
                        ("ident", ident, ident_in)):
        DMA("sync", "w", nm, sb[:, :], None, src[:, :], rot=wnext())

    t = P.op("gpsimd", "memset", smbias[0:1, :], 0.0)
    T.did_write("smbias", t)
    DMA("sync", "w", "tfd", tiny[0:1, 0:16], None, fdb_in[:, :], rot=wnext())
    DMA("sync", "w", "smf0", smf2[0][0:1, 0:16], None, sdb_in[:, :], rot=wnext())
    tok = V("tensor_copy", "smbias", smbias[0:1, 16:32], [("tfd", tiny[0:1, 0:16])])
    tok = V("tensor_copy", "smbias", smbias[0:1, 32:48], [("smf0", smf2[0][0:1, 0:16])])

    bigs = []
    for kc in range(8):
        b = kc * 2128
        bigs.append(("wk", kc, f"wmov_k{kc}", wmov[:, b:b + 1024]))
        bigs.append(("wv", kc, f"wmov_v{kc}", wmov[:, b + 1024:b + 2048]))
    for kc in range(8):
        bigs.append(("wq", kc, f"wq{kc}", wqfull[:, kc * DM:(kc + 1) * DM]))
    queues = [("sync", "w"), ("sync", "tq"), ("scalar", "tp")]
    for i, (wn, kc, nm, dst) in enumerate(bigs):
        xb, xn = xbuf[i % 2], f"xbuf{i % 2}"
        e, q = queues[i % 3]
        DMA(e, q, xn, xb[:, :], None,
            w_in[wn][kc * 128:(kc + 1) * 128, :], rot=i // 3)
        V("tensor_copy", nm, dst, [(xn, xb[:, :])],
          eng=("vector" if i % 2 == 0 else "scalar"))
    for kc in range(8):
        b = kc * 2128
        sb_, sn = smf2[kc % 2], f"smf{kc % 2}"
        for si, snm in enumerate(("wb", "wfd", "wsd", "wfg", "wsg")):
            DMA("gpsimd", "sm", sn, sb_[:, si * 16:(si + 1) * 16], None,
                sm_in[snm][kc * 128:(kc + 1) * 128, :], rot=kc,
                fresh=(si == 0))
        V("tensor_copy", f"wmov_s{kc}", wmov[:, b + 2048:b + 2128],
          [(sn, sb_[:, :])])
    wmov_toks = [T.w[f"wmov_k{k}"] for k in range(8)] + \
        [T.w[f"wmov_v{k}"] for k in range(8)] + \
        [T.w[f"wmov_s{k}"] for k in range(8)]

    I32 = mybir.dt.int32
    MAGIC = 0x5f3759df

    def rsqrt_chain(a_name, a_ap, r_name, r_ap, t_name, t_ap, iters=2,
                    also=()):
        V("tensor_scalar_max", a_name, a_ap, [(a_name, a_ap)], 1e-30,
          also_read=also)
        V("tensor_single_scalar", r_name, r_ap.bitcast(I32),
          [(a_name, a_ap.bitcast(I32))], 1, ALU.arith_shift_right)
        V("tensor_scalar", r_name, r_ap.bitcast(I32),
          [(r_name, r_ap.bitcast(I32))], -1, MAGIC, ALU.mult, ALU.add)
        for _ in range(iters):
            V("tensor_mul", t_name, t_ap, [(r_name, r_ap), (r_name, r_ap)])
            V("tensor_mul", t_name, t_ap, [(t_name, t_ap), (a_name, a_ap)])
            V("tensor_scalar", t_name, t_ap, [(t_name, t_ap)], -0.5, 1.5,
              ALU.mult, ALU.add)
            V("tensor_mul", r_name, r_ap, [(r_name, r_ap), (t_name, t_ap)])

    smb_bc2 = [smb2[i][:, 0:16].unsqueeze(2).broadcast_to([128, 16, 64])
               for i in range(2)]
    pbf_bc = pbf[:, 0:16].unsqueeze(2).broadcast_to([128, 16, 64])

    C_RK, C_RV, C_DOT = 0, 16, 32
    C_S0, C_SRK = 48, 80
    C_DEN, C_RDEN, C_SCR, C_PSI, C_SUR = 112, 128, 144, 160, 176

    def pc(c0):
        return psich[:, c0:c0 + 16]

    def refined_sqrt(src_name, dst_name, dst_c):
        V("activation", "p_s0", pc(C_S0), [(src_name, pc({"psi_rk": C_RK,
          "psi_rv": C_RV}[src_name]))], AF.Sqrt, eng="scalar")
        V("reciprocal", "p_tmp", pc(C_TMP), [("p_s0", pc(C_S0))])
        V("tensor_mul", "p_tmp", pc(C_TMP),
          [(src_name, pc({"psi_rk": C_RK, "psi_rv": C_RV}[src_name])),
           ("p_tmp", pc(C_TMP))])
        V("tensor_add", "p_tmp", pc(C_TMP),
          [("p_s0", pc(C_S0)), ("p_tmp", pc(C_TMP))])
        V("tensor_scalar", dst_name, pc(dst_c), [("p_tmp", pc(C_TMP))],
          0.5, 1e-8, ALU.mult, ALU.add)

    kk2v = [kk2[i].ap().rearrange("p (h j d) -> p h j d", h=16, j=2)
            for i in range(3)]
    vbtv = [vbt[i].ap().rearrange("p (h d) -> p h d", h=16) for i in range(3)]
    junkv3 = [psj3[i].ap().rearrange("p (h d) -> p h d", h=16)
              for i in range(2)]

    def st_xload(c):
        DMA("sync", "x", f"xbuf{c % 2}", xbuf[c % 2][:, :], None,
            x_in[c * 128:(c + 1) * 128, :], rot=c)

    def st_lnxn(c):
        xname = f"xbuf{c % 2}"
        xb = xbuf[c % 2]
        xn_tm, xn_name = xn_tm2[c % 2], f"xn_tm{c % 2}"
        V("bn_stats", "bnb", bnb[:, 0:6], [(xname, xb[:, 0:512])])
        V("bn_stats", "bnb", bnb[:, 6:12], [(xname, xb[:, 512:1024])])
        V("bn_aggr", "t01", tiny[:, 0:2], [("bnb", bnb[:, 0:12])])
        V("tensor_scalar", "t2", tiny[:, 2:3], [("t01", tiny[:, 1:2])],
          1e-5, None, ALU.add)
        rsqrt_chain("t2", tiny[:, 2:3], "t1r", tiny[:, 1:2],
                    "t4", tiny[:, 4:5])
        V("tensor_scalar", xn_name, xn_tm[:, :], [(xname, xb[:, :])],
          tiny[:, 0:1], tiny[:, 1:2], ALU.subtract, ALU.mult,
          also_read=("t1r", "t01"))
        tap(f"xn{c}", xn_name, xn_tm[:, :])

    def st_xT(c):
        s4, cpos = c // SUP, c % SUP
        sl = xnsl[s4 % 2]
        slname = f"sl{s4 % 2}c{cpos}"
        xn_tm, xn_name = xn_tm2[c % 2], f"xn_tm{c % 2}"
        mms = []
        for kc in range(8):
            mms.append((Sps[:, kc * 128:(kc + 1) * 128],
                        xn_tm[:, kc * 128:(kc + 1) * 128],
                        ident[:, :], True, True, None))
        mm_group(mms, reads=[xn_name, "ident"], writes=["Sps"])
        V("tensor_copy", slname,
          bass.AP(sl, cpos * 128,
                  [[8 * STOK, 128], [STOK, 8], [1, 128]]),
          [("Sps", Sps.ap().rearrange("p (b t) -> p b t", b=8))],
          eng="vector")
        if "noxc" not in skip:
            DMA("sync", "xc", f"xnfmd{c}", xnfm_d[c, :, :], slname,
                bass.AP(sl, cpos * 128, [[8 * STOK, 128], [STOK, 8], [1, 128]]),
                rot=c)

    def st_kvsm(c):
        s4, cpos = c // SUP, c % SUP
        sl = xnsl[s4 % 2]
        slname = f"sl{s4 % 2}c{cpos}"
        mms = []
        for kc in range(8):
            st0, sp7 = (kc == 0), (kc == 7)
            lhs = sl[:, kc * STOK + cpos * 128: kc * STOK + cpos * 128 + 128]
            b = kc * 2128
            mms += [
                (kps[:, 0:512], lhs, wmov[:, b:b + 512], st0, sp7, None),
                (kps[:, 512:1024], lhs, wmov[:, b + 512:b + 1024], st0, sp7, None),
                (vps[:, 0:512], lhs, wmov[:, b + 1024:b + 1536], st0, sp7, None),
                (vps[:, 512:1024], lhs, wmov[:, b + 1536:b + 2048], st0, sp7, None),
                (smps[:, 0:NSM], lhs, wmov[:, b + 2048:b + 2128], st0, False, None),
            ]
        mms.append((smps[:, 0:NSM], ones_col[:, :], smbias[0:1, :],
                    False, True, None))
        mm_group(mms, reads=[slname, "wmov", "ones_col", "smbias"],
                 writes=["kps", "vps", "smps"],
                 extra_waits=wmov_toks if c == 0 else ())

    def st_epi(c):
        p = c % 3
        sp = c % 2
        sg = c % 3
        ka, vn = f"kk2a{p}", f"vbt{p}"
        smfn, smbn, smgn = f"smf{sp}", f"smb{sp}", f"smg{sg}"
        V("activation", smfn, smf2[sp][:, :], [("smps", smps[:, 0:NSM])],
          AF.Tanh, scale=0.5, eng="scalar")
        V("activation", ka, kk2v[p][:, :, 0, :], [("kps", kps[:, :])], AF.Silu,
          eng="scalar")
        V("activation", vn, vbtv[p], [("vps", vps[:, :])], AF.Silu,
          eng="scalar")
        V("activation", smbn, smb2[sp][:, :], [(smfn, smf2[sp][:, 0:16])],
          AF.Copy, bias=0.5, scale=0.5, eng="scalar")
        V("activation", smgn, smg2[sg][:, :], [(smfn, smf2[sp][:, :])],
          AF.Copy, bias=0.5, scale=0.5, eng="scalar")
        tap(f"smf{c}", smgn, smg2[sg][:, :])
        V("tensor_tensor", ka, kk2v[p][:, :, 0, :],
          [(ka, kk2v[p][:, :, 0, :]), (smbn, smb_bc2[sp])], ALU.mult,
          eng="gpsimd")
        V("tensor_tensor", vn, vbtv[p],
          [(vn, vbtv[p]), (smbn, smb_bc2[sp])], ALU.mult, eng="gpsimd")
        tap(f"vb{c}", vn, vbt[p][:, :])

    def st_psi1(c):
        p = c % 3
        rb = 0 if c % 2 == 0 else 192
        ka, vn = f"kk2a{p}", f"vbt{p}"
        kbv = kk2v[p][:, :, 0, :]
        V("activation", "psj0", junkv3[0], [(ka, kbv)], AF.Square,
          eng="scalar")
        V("activation", "psj1", junkv3[1], [(vn, vbtv[p])], AF.Square,
          eng="scalar")
        rk = psich[:, rb + C_RK:rb + C_RK + 16]
        rv = psich[:, rb + C_RV:rb + C_RV + 16]
        dot = psich[:, rb + C_DOT:rb + C_DOT + 16]
        V("reduce_sum", f"psi_rk{c % 2}", rk, [("psj0", junkv3[0])], AXX,
          also_read=(f"psi_rkv{c % 2}",))
        V("reduce_sum", f"psi_rv{c % 2}", rv, [("psj1", junkv3[1])], AXX,
          also_read=(f"psi_rkv{c % 2}",))
        V("tensor_tensor", "psj0", junkv3[0], [(ka, kbv), (vn, vbtv[p])],
          ALU.mult, eng="gpsimd")
        V("reduce_sum", f"psi_dot{c % 2}", dot, [("psj0", junkv3[0])], AXX)

    def st_psi2(c):
        p = c % 3
        sp = c % 2
        sg = c % 3
        rb = 0 if c % 2 == 0 else 192
        rk = psich[:, rb + C_RK:rb + C_RK + 16]
        rv = psich[:, rb + C_RV:rb + C_RV + 16]
        dot = psich[:, rb + C_DOT:rb + C_DOT + 16]
        ka = f"kk2a{p}"
        kbv = kk2v[p][:, :, 0, :]
        V("tensor_mul", "p_den", pc(C_DEN),
          [(f"psi_rk{sp}", rk), (f"psi_rv{sp}", rv)])
        rsqrt_chain("p_den", pc(C_DEN), "p_rden", pc(C_RDEN),
                    "p_t2", psich[:, C_SRK:C_SRK + 16])
        V("activation", "p_scr", pc(C_SCR), [(f"psi_dot{sp}", dot)], AF.Abs,
          eng="scalar")
        V("tensor_mul", "p_scr", pc(C_SCR), [("p_scr", pc(C_SCR)),
                                             ("p_rden", pc(C_RDEN))])
        V("activation", "p_psi", pc(C_PSI), [("p_scr", pc(C_SCR))], AF.Tanh,
          scale=1.5, eng="scalar")
        V("activation", "p_sur", pc(C_SUR), [("p_psi", pc(C_PSI))], AF.Tanh,
          scale=2.5, eng="scalar")
        V("activation", "pbf", pbf[:, :], [("p_sur", pc(C_SUR))], AF.Square,
          bias=cbias[:, 1:2], scale=0.5, eng="scalar", also_read=("cbias",))
        tap(f"psi{c}", (f"psi_rkv{sp}", f"psi_dot{sp}", "p_r2", "p_t2",
                        "p_den", "p_rden", "p_scr", "p_psi", "p_sur",
                        "p_f1", "p_f2", "p_f3", "p_f4"),
            psich[:, rb:rb + 192] if rb == 0 else psich[:, 0:192])
        V("tensor_tensor", f"kk2b{p}", kk2v[p][:, :, 1, :],
          [(ka, kbv), ("pbf", pbf_bc)], ALU.mult, eng="gpsimd")
        tap(f"kb{c}", (ka, f"kk2b{p}"), kk2[p][:, :])
        folds = [("p_f1", C_S0, -0.05, 0.95), ("p_f2", C_S0 + 16, -0.025,
                 0.975), ("p_f3", C_SRK, 0.15, 0.65), ("p_f4", C_SRK + 16,
                 -0.15, 0.35)]
        for fn, col, a, b in folds:
            V("activation", fn, psich[:, col:col + 16],
              [("p_psi", pc(C_PSI))], AF.Copy, bias=b, scale=a, eng="scalar")
        tok = V("tensor_mul", f"psi_rkv{sp}", rk,
                [("p_f1", psich[:, C_S0:C_S0 + 16]),
                 (f"smg{sg}", smg2[sg][:, 16:32])],
                also_read=(f"psi_rk{sp}",), eng="gpsimd")
        T.did_write(f"psi_rk{sp}", tok)
        tok = V("tensor_mul", f"psi_rkv{sp}", rv,
                [("p_f2", psich[:, C_S0 + 16:C_S0 + 32]),
                 (f"smg{sg}", smg2[sg][:, 32:48])],
                also_read=(f"psi_rv{sp}",), eng="gpsimd")
        T.did_write(f"psi_rv{sp}", tok)
        tok = V("tensor_copy", f"lam2b{sp}", lam2b[sp][:, :],
                [(f"psi_rkv{sp}", psich[:, rb + C_RK:rb + C_RK + 32])],
                eng="gpsimd")
        T.did_read(f"psi_rk{sp}", tok)
        T.did_read(f"psi_rv{sp}", tok)
        V("tensor_mul", f"ab{c}", AB[:, 32 * c:32 * c + 16],
          [("p_f3", psich[:, C_SRK:C_SRK + 16]),
           (f"smg{sg}", smg2[sg][:, 48:64])], eng="gpsimd")
        V("tensor_mul", f"ab{c}b", AB[:, 32 * c + 16:32 * c + 32],
          [("p_f4", psich[:, C_SRK + 16:C_SRK + 32]),
           (f"smg{sg}", smg2[sg][:, 64:80])], eng="gpsimd")

    def st_wdsm(c):
        p = c % 2
        mm_group([(mps[0:32, 0:1], lam2b[p][:, :], ones_mov[:, :], True, True,
                   None)],
                 reads=[f"lam2b{p}", "ones_mov"], writes=["mps"])
        b0 = 2 * p
        V("activation", "d32", d32[:, b0:b0 + 1], [("mps", mps[0:32, 0:1])],
          AF.Copy, bias=0.0, scale=1.0 / 128.0, eng="scalar")
        for i in range(7):
            a, bcol = b0 + i % 2, b0 + (i + 1) % 2
            V("activation", "d32", d32[:, bcol:bcol + 1],
              [("d32", d32[:, a:a + 1])], AF.Square, eng="scalar")
        dcol = b0 + 1
        dm = d32m[:, 16 * p:16 * p + 16]
        V("tensor_scalar", f"d32m{p}", dm, [("msk32", msk32[:, :])],
          d32[:, dcol:dcol + 1], None, ALU.mult, also_read=("d32",))
        mm_group([(mps[:, 16:32], sel32[:, :], dm, True, True, None)],
                 reads=["sel32", f"d32m{p}"], writes=["mps2"])
        V("tensor_copy", f"dall{c}", Dall[:, 16 * c:16 * c + 16],
          [("mps2", mps[:, 16:32])], eng="scalar")
        tap(f"dsm{c}", f"dall{c}", Dall[:, 16 * c:16 * c + 16])

    def st_S(c):
        p = c % 3
        mms = []
        for j in range(8):
            h1, h2 = 2 * j, 2 * j + 1
            mms.append((Sps[0:64, j * 128:(j + 1) * 128],
                        vbt[p][:, h1 * 64:(h1 + 1) * 64],
                        kk2[p][:, h1 * 128:(h1 + 1) * 128], True, True, (0, 0)))
            mms.append((Sps[64:128, j * 128:(j + 1) * 128],
                        vbt[p][:, h2 * 64:(h2 + 1) * 64],
                        kk2[p][:, h2 * 128:(h2 + 1) * 128], True, True, (0, 64)))
        mm_group(mms, reads=[f"vbt{p}", f"kk2a{p}", f"kk2b{p}"], writes=["Sps"])

    def st_chain(c):
        sn = f"snapR{c % 2}"
        snap_c = snapR[c % 2][:, :]
        if c == 0:
            V("tensor_copy", sn, snap_c, [("Sps", Sps[:, :])])
        else:
            dview = Dall[:, 16 * c:16 * c + 16] \
                .rearrange("p (j s) -> p j s", j=8) \
                .unsqueeze(3).broadcast_to([128, 8, 2, 64])
            prev = snapR[(c - 1) % 2].ap() \
                .rearrange("p (j s e) -> p j s e", j=8, s=2)
            V("tensor_tensor", sn,
              snap_c.rearrange("p (j s e) -> p j s e", j=8, s=2),
              [(f"snapR{(c - 1) % 2}", prev), (f"dall{c}", dview)], ALU.mult,
              eng="gpsimd")
            V("tensor_add", sn, snap_c, [(sn, snap_c), ("Sps", Sps[:, :])])
        tap(f"snap{c}", sn, snap_c)

    def st_snapst(c):
        DMA("sync", "tp", f"snapd{c}", snap_d[c, :, :], f"snapR{c % 2}",
            snapR[c % 2][:, :], rot=c)

    def st_q(s):
        banks = [kps[:, 0:512], kps[:, 512:1024], vps[:, 0:512],
                 vps[:, 512:1024], smps[:, 0:512], Sps[:, 0:512],
                 Sps[:, 512:1024], mps[:, 0:512]]
        bank_names = ["kps", "kps", "vps", "vps", "smps", "Sps", "Sps", "mps"]
        bank_all = ["kps", "vps", "smps", "Sps", "mps", "mps2"]
        sl = xnsl[s % 2]
        slnames = [f"sl{s % 2}c{i}" for i in range(SUP)]
        for kc in range(8):
            mms = []
            for m in range(8):
                mov = wmov[:, 0:STOK] if "qdummy" in skip \
                    else sl[:, kc * STOK:(kc + 1) * STOK]
                mms.append((banks[m],
                            wqfull[:, kc * DM + m * 128:kc * DM + (m + 1) * 128],
                            mov, kc == 0, kc == 7, None))
            rds = (["wmov"] if "qdummy" in skip else slnames) + \
                [f"wq{kc}"]
            grp = mm_group(mms, reads=rds,
                           writes=(bank_all if kc == 0 else []),
                           extra_waits=([T.w.get("qgrp")] if kc > 0 else ()))
            T.did_write("qgrp", grp)
            if kc > 0:
                for n in bank_all:
                    T.did_write(n, grp)
        if "qnosil" in skip:
            return
        for m in range(8):
            V("activation", f"qfm{m}",
              qfm[:, m * NTOK + s * STOK: m * NTOK + (s + 1) * STOK],
              [(bank_names[m], banks[m])], AF.Silu, eng="scalar")
        tap(f"q{s}", "qfm0",
            qfm[:, 0 * NTOK + s * STOK: 0 * NTOK + (s + 1) * STOK])

    st_xload(0)
    st_lnxn(0)
    st_xT(0)
    st_xload(1)
    st_lnxn(1)
    st_xload(2)
    for c in range(NCHUNK + 2):
        if c + 3 < NCHUNK:
            st_xload(c + 3)
        if c + 2 < NCHUNK:
            st_lnxn(c + 2)
        if c >= 3:
            st_snapst(c - 3)
        if 0 <= c - 2 < NCHUNK:
            st_psi2(c - 2)
        if c < NCHUNK:
            st_kvsm(c)
            st_epi(c)
        if c + 1 < NCHUNK:
            st_xT(c + 1)
        if c < NCHUNK:
            st_psi1(c)
        if c >= 2:
            st_wdsm(c - 2)
            st_S(c - 2)
            st_chain(c - 2)
        if c >= 1 and (c - 1) % SUP == SUP - 1 and "q" not in skip:
            st_q((c - 1) // SUP)
    st_snapst(NCHUNK - 1)

    if use_cc:
        DMA("gpsimd", "gp", "cc_in", cc_in_d[:, :], f"snapR{(NCHUNK - 1) % 2}",
            snapR[(NCHUNK - 1) % 2][:, :], rot=0)
        t_cc = P.op("gpsimd", "collective_compute", "AllGather", ALU.bypass,
                    [[i for i in range(NCORE)]],
                    [cc_in_d.ap().opt()], [cc_out_d.ap().opt()],
                    waits=T.rw("cc_in") + [T.w.get("cc_in")])
        T.did_write("cc_out", t_cc)

    P.fence_all()

    for nm, sb, src in (("selD", selD, selD_in), ("selG", selG, selG_in),
                        ("mvec", mvec, mvec_in)):
        DMA("sync", "w", nm, sb[:, :], None, src[:, :], rot=wnext())
    DMA("sync", "w", "wsD", wstageD[0:1, 0:512], None, fbr_in[:, :],
        rot=wnext())
    V("tensor_copy", "fbias", fbias[0:1, 0:512], [("wsD", wstageD[0:1, 0:512])])
    DMA("sync", "w", "wsD", wstageD[0:1, 0:512], None, fbi_in[:, :],
        rot=wnext())
    V("tensor_copy", "fbias2", fbias[0:1, 512:1024],
      [("wsD", wstageD[0:1, 0:512])])

    for kc in range(8):
        DMA("sync", "w", "wsD", wstageD[:, :], None,
            w_in["wo"][kc * 128:(kc + 1) * 128, :], rot=wnext())
        V("tensor_copy", f"wo{kc}", wo_b[:, kc * 1024:(kc + 1) * 1024],
          [("wsD", wstageD[:, :])], eng="scalar")
    wo_toks = [T.w[f"wo{k}"] for k in range(8)]

    for which, g1src, g2src, dest, dname in (
            ("r", g1r_in, g2r_in, fwrt, "fwrt"), ("i", g1i_in, g2i_in, fwit,
                                                  "fwit")):
        for c4 in range(4):
            DMA("sync", "w", "g1f", g1f[:, :], None,
                bass.AP(g1src, c4 * 32, [[1, 32], [128, 32]]),
                allow_slow_non_contiguous=True, rot=wnext())
            DMA("sync", "w", "g2f", g2f[:, :], None,
                bass.AP(g2src, c4 * 256, [[1, 16], [16, 16]]),
                allow_slow_non_contiguous=True, rot=wnext())
            V("tensor_copy", "g1t", g1t[:, :], [("g1f", g1f[:, :])])
            V("tensor_copy", "g2t", g2t[:, :], [("g2f", g2f[:, :])])
            mm_group([(mps[:, 32:48], selG[:, :], g2t[:, :], True, True, None)],
                     reads=["selG", "g2t"], writes=["mpsB"])
            V("tensor_copy", "g2s", g2s[:, :], [("mpsB", mps[:, 32:48])])
            for t4 in range(4):
                mm_group([(mps[:, 0:32], selD[:, t4 * 128:(t4 + 1) * 128],
                           g1t[:, :], True, True, None)],
                         reads=["selD", "g1t"], writes=["mpsA"])
                V("tensor_copy", "g1s", g1s[:, :], [("mpsA", mps[:, 0:32])])
                a_bc = g1s.ap().unsqueeze(2).broadcast_to([128, 32, 16])
                b_bc = g2s.ap().unsqueeze(1).broadcast_to([128, 32, 16])
                dst = dest[:, t4 * 512:(t4 + 1) * 512] \
                    .rearrange("p (b e) -> p b e", b=32)
                if c4 == 0:
                    V("tensor_tensor", f"{dname}{t4}", dst,
                      [("g1s", a_bc), ("g2s", b_bc)], ALU.mult)
                else:
                    V("tensor_tensor", "fttmp",
                      fttmp.ap().rearrange("p (b e) -> p b e", b=32),
                      [("g1s", a_bc), ("g2s", b_bc)], ALU.mult)
                    V("tensor_add", f"{dname}{t4}", dst,
                      [(f"{dname}{t4}", dst), ("fttmp", fttmp.ap()
                        .rearrange("p (b e) -> p b e", b=32))])
    for kc in range(8):
        b = kc * 1024
        if kc < 4:
            V("tensor_copy", f"wffn{kc}a", wffn[:, b:b + 512],
              [(f"fwrt{kc}", fwrt[:, kc * 512:(kc + 1) * 512])])
            V("tensor_copy", f"wffn{kc}b", wffn[:, b + 512:b + 1024],
              [(f"fwit{kc}", fwit[:, kc * 512:(kc + 1) * 512])], eng="scalar")
        else:
            V("tensor_scalar", f"wffn{kc}a", wffn[:, b:b + 512],
              [(f"fwit{kc - 4}", fwit[:, (kc - 4) * 512:(kc - 3) * 512])],
              -1.0, None, ALU.mult)
            V("tensor_copy", f"wffn{kc}b", wffn[:, b + 512:b + 1024],
              [(f"fwrt{kc - 4}", fwrt[:, (kc - 4) * 512:(kc - 3) * 512])],
              eng="scalar")
    wffn_toks = [T.w[f"wffn{k}a"] for k in range(8)] + \
        [T.w[f"wffn{k}b"] for k in range(8)]

    if not use_cc:
        t = P.op("gpsimd", "memset", state_in[:, :], 0.0)
        T.did_write("state_in", t)
    for j in range(NCORE if use_cc else 0):
        DMA("gpsimd", "gp", "gath0", gath[0][:, :], "cc_out",
            cc_out_d[j * 128:(j + 1) * 128, :], rot=j)
        pp = [corr[0], state_in][j % 2]
        ppn = ["corr0", "state_in"][j % 2]
        prevn = ["corr0", "state_in"][(j + 1) % 2]
        if j == 0:
            V("tensor_scalar", ppn, pp[:, :], [("gath0", gath[0][:, :])],
              mvec[:, 0:1], None, ALU.mult, also_read=("mvec",))
        else:
            V("scalar_tensor_tensor", ppn, pp[:, :],
              [("gath0", gath[0][:, :])],
              mvec[:, j:j + 1],
              [corr[0], state_in][(j + 1) % 2][:, :],
              ALU.mult, ALU.add, also_read=(prevn, "mvec"))
    tap("state_in", "state_in", state_in[:, :])
    P.fence_all()

    kps8 = kps.ap().rearrange("p (h j d) -> p h j d", h=8, j=2)
    vps8 = vps.ap().rearrange("p (h j d) -> p h j d", h=8, j=2)

    def st_xD(c):
        DMA("sync", "x", f"xbufD{c % 2}", xbufD[c % 2][:, :], None,
            x_in[c * 128:(c + 1) * 128, :], rot=c)

    def st_xnfm(c):
        DMA("sync", "xc", f"xnfmD{c % 2}", xnfmD[c % 2][:, :], f"xnfmd{c}",
            xnfm_d[c, :, :], rot=c)

    def st_snapld(c):
        DMA("sync", "tq", f"snapD{c % 2}", snapD[c % 2][:, :], f"snapd{c}",
            snap_d[c, :, :], rot=c)

    def st_corr(c):
        dview = Dall[:, 16 * c:16 * c + 16] \
            .rearrange("p (j s) -> p j s", j=8) \
            .unsqueeze(3).broadcast_to([128, 8, 2, 64])
        cb, cbn = corr[c % 2], f"corr{c % 2}"
        pv, pvn = (state_in, "state_in") if c == 0 else \
            (corr[(c + 1) % 2], f"corr{(c + 1) % 2}")
        tok = V("tensor_tensor", cbn,
                cb.ap().rearrange("p (j s e) -> p j s e", j=8, s=2),
                [(pvn, pv.ap().rearrange("p (j s e) -> p j s e", j=8, s=2)),
                 (f"dall{c}", dview)], ALU.mult)
        if c == 0:
            T.did_read("outb0", tok)
        V("tensor_add", "snapt", snapt[:, :],
          [(f"snapD{c % 2}", snapD[c % 2][:, :]), (cbn, cb[:, :])])

    def st_ffn(c):
        p = c % 2
        mms = []
        for kc in range(8):
            st0 = kc == 0
            lhs = xnfmD[p][:, kc * 128:(kc + 1) * 128]
            b = kc * 1024
            mms.append((smps[:, 0:512], lhs, wffn[:, b:b + 512], st0, False,
                        None))
            mms.append((mps[:, 0:512], lhs, wffn[:, b + 512:b + 1024], st0,
                        False, None))
        mms.append((smps[:, 0:512], ones_col[:, :], fbias[0:1, 0:512], False,
                    True, None))
        mms.append((mps[:, 0:512], ones_col[:, :], fbias[0:1, 512:1024], False,
                    True, None))
        mm_group(mms, reads=[f"xnfmD{p}", "ones_col", "fbias", "fbias2"],
                 writes=["smps", "mps", "mpsA", "mpsB"],
                 extra_waits=(wffn_toks if c == 0 else ()))
        V("activation", f"gbuf{p}", gbuf[p][:, 0:512],
          [("smps", smps[:, 0:512])], AF.Silu, eng="scalar")
        V("activation", f"gbuf{p}", gbuf[p][:, 512:1024],
          [("mps", mps[:, 0:512])], AF.Silu, eng="scalar")
        tap(f"g{c}", f"gbuf{p}", gbuf[p][:, :])

    def st_o(c):
        mms = []
        for j in range(8):
            for hh in range(2):
                rows = slice(hh * 64, hh * 64 + 64)
                outap = [kps, vps][hh][:, j * 128:(j + 1) * 128]
                mms.append((outap,
                            qfm[rows, j * NTOK + c * 128: j * NTOK + (c + 1) * 128],
                            snapt[rows, j * 128:(j + 1) * 128],
                            True, True, None))
        mm_group(mms, reads=["snapt"] + [f"qfm{j}" for j in range(8)],
                 writes=["kps", "vps"])

    def st_gate(c):
        p = c % 2
        on = f"o_comb{p}"
        for par, psv, psn in ((0, kps8, "kps"), (1, vps8, "vps")):
            a_bc = bass.AP(AB, 32 * c + par,
                           [[AB.shape[1], 128], [2, 8], [0, 64]])
            b_bc = bass.AP(AB, 32 * c + 16 + par,
                           [[AB.shape[1], 128], [2, 8], [0, 64]])
            oc = bass.AP(o_comb[p], par * 64,
                         [[o_comb[p].shape[1], 128], [128, 8], [1, 64]])
            tmp = bass.AP(o_fm, par * 64,
                          [[o_fm.shape[1], 128], [128, 8], [1, 64]])
            V("tensor_tensor", on, oc,
              [(psn, psv[:, :, 0, :]), (f"ab{c}", a_bc)], ALU.mult)
            V("tensor_tensor", "o_fm", tmp,
              [(psn, psv[:, :, 1, :]), (f"ab{c}b", b_bc)], ALU.mult)
            V("tensor_add", on, oc, [(on, oc), ("o_fm", tmp)])
        tap(f"oc{c}", on, o_comb[p][:, :])

    def st_oT(c):
        p = c % 2
        mms = []
        for kc in range(8):
            mms.append((kps[:, kc * 128:(kc + 1) * 128],
                        o_comb[p][:, kc * 128:(kc + 1) * 128],
                        ident[:, :], True, True, None))
        mm_group(mms, reads=[f"o_comb{p}", "ident"], writes=["kps"])
        V("tensor_copy", "o_fm", o_fm[:, 0:512], [("kps", kps[:, 0:512])],
          eng="scalar")
        V("tensor_copy", "o_fm", o_fm[:, 512:1024], [("kps", kps[:, 512:1024])],
          eng="scalar")

    def st_wo(c):
        mms = []
        for kc in range(8):
            st0, sp7 = kc == 0, kc == 7
            lhs = o_fm[:, kc * 128:(kc + 1) * 128]
            b = kc * 1024
            mms.append((Sps[:, 0:512], lhs, wo_b[:, b:b + 512], st0, sp7, None))
            mms.append((Sps[:, 512:1024], lhs, wo_b[:, b + 512:b + 1024], st0,
                        sp7, None))
        mm_group(mms, reads=["o_fm"] + ([f"wo{k}" for k in range(8)]
                                        if c == 0 else []),
                 writes=["Sps"])

    def st_final(c):
        p = c % 2
        ob, obn = outb[p], f"outb{p}"
        V("tensor_tensor", obn, ob[:, :],
          [(f"gbuf{p}", gbuf[p][:, :]), ("Sps", Sps[:, :])], ALU.mult)
        V("tensor_add", obn, ob[:, :], [(obn, ob[:, :]),
                                        (f"xbufD{p}", xbufD[p][:, :])])
        DMA("sync", "st", f"out{c}", out_d[c * 128:(c + 1) * 128, :], obn,
            ob[:, :], rot=c)

    st_snapld(0)
    st_xnfm(0)
    st_corr(0)
    for c in range(0 if "d" not in skip else NCHUNK + 2, NCHUNK + 2):
        if c + 1 < NCHUNK:
            st_snapld(c + 1)
        if 0 <= c - 1 < NCHUNK:
            st_xD(c - 1)
        if c + 1 < NCHUNK:
            st_xnfm(c + 1)
        if c - 2 >= 0:
            st_final(c - 2)
        if c < NCHUNK:
            st_ffn(c)
        if c - 1 >= 0 and c - 1 < NCHUNK:
            st_oT(c - 1)
            st_wo(c - 1)
        if c < NCHUNK:
            st_o(c)
            st_gate(c)
            if c + 1 < NCHUNK:
                st_corr(c + 1)

    if "d" in skip:
        t = P.op("gpsimd", "memset", outb[0][:, :], 0.0)
        T.did_write("outb0", t)
        for c in range(NCHUNK):
            DMA("sync", "st", f"out{c}", out_d[c * 128:(c + 1) * 128, :],
                "outb0", out_d is not None and outb[0][:, :], rot=c, fresh=(c < 2))
    P.final_wait("sync", "st")
    P.final_wait("gpsimd", "gp")

    with nc.Block() as block, es:
        P.build(block)
    return nc, tap_outs



def make_consts(core):
    sel32 = np.zeros((32, 128), dtype=bf16)
    for k in range(32):
        h = k % 16
        par = h % 2
        sel32[k, par * 64:(par + 1) * 64] = 1.0
    sel32[:] = 0
    for k in range(32):
        h = k % 16
        for p in range(128):
            if h % 2 == p // 64:
                sel32[k, p] = 1.0
    msk32 = np.zeros((32, 16), dtype=np.float32)
    for k in range(32):
        s, h = k // 16, k % 16
        j = h // 2
        msk32[k, j * 2 + s] = 1.0
    mvec = np.zeros((128, 8), dtype=np.float32)
    if core % 2 == 1:
        mvec[:, core - 1] = 1.0
    selD = np.zeros((32, 512), dtype=bf16)
    for t4 in range(4):
        for p in range(128):
            selD[8 * t4 + p // 16, t4 * 128 + p] = 1.0
    selG = np.zeros((16, 128), dtype=bf16)
    for p in range(128):
        selG[p % 16, p] = 1.0
    return sel32, msk32, mvec, selD, selG


def core_inputs(core, inp):
    b, half = core // 2, core % 2
    x = np.ascontiguousarray(
        np.asarray(inp["L0_attn_hidden_states"])[b, half * NTOK:(half + 1) * NTOK, :])
    sel32, msk32, mvec, selD, selG = make_consts(core)
    asF = lambda a: np.ascontiguousarray(np.asarray(a, dtype=np.float32))
    return {
        "x": x.astype(np.float32),
        "wq": asF(inp["L0_attn_Wq"]), "wk": asF(inp["L0_attn_Wk"]),
        "wv": asF(inp["L0_attn_Wv"]), "wo": asF(inp["L0_attn_Wo"]),
        "wb": asF(inp["L0_attn_Wb"]), "wfd": asF(inp["L0_attn_Wfd"]),
        "wsd": asF(inp["L0_attn_Wsd"]), "wfg": asF(inp["L0_attn_Wfg"]),
        "wsg": asF(inp["L0_attn_Wsg"]),
        "fdb": asF(inp["L0_attn_fd_bias"]).reshape(1, H),
        "sdb": asF(inp["L0_attn_sd_bias"]).reshape(1, H),
        "g1r": asF(inp["L0_ffn_G1_R"]).reshape(32, 4, 32),
        "g2r": asF(inp["L0_ffn_G2_R"]).reshape(4, 16, 16),
        "g1i": asF(inp["L0_ffn_G1_I"]).reshape(32, 4, 32),
        "g2i": asF(inp["L0_ffn_G2_I"]).reshape(4, 16, 16),
        "fbr": asF(inp["L0_ffn_bias_real"]).reshape(1, 512),
        "fbi": asF(inp["L0_ffn_bias_imag"]).reshape(1, 512),
        "sel32c": sel32, "msk32c": msk32, "mvecc": mvec,
        "selDc": selD, "selGc": selG,
        "identc": np.eye(128, dtype=bf16),
    }


def kernel(**inputs):
    nc, _ = build_nc()
    in_maps = [core_inputs(i, inputs) for i in range(NCORE)]
    res = run_bass_kernel_spmd(nc, in_maps, core_ids=list(range(NCORE)))
    outs = [res.results[i]["out"] for i in range(NCORE)]
    full = np.empty((4, 8192, DM), dtype=np.float32)
    for i in range(NCORE):
        b, half = i // 2, i % 2
        full[b, half * NTOK:(half + 1) * NTOK, :] = outs[i]
    return full
